# version 21
# baseline (speedup 1.0000x reference)
import numpy as np
import ml_dtypes
from contextlib import ExitStack

import concourse.bass as bass
import concourse.mybir as mybir
from concourse.bass_utils import run_bass_kernel_spmd

F32 = mybir.dt.float32
BF16 = mybir.dt.bfloat16
AF = mybir.ActivationFunctionType
ALU = mybir.AluOpType

D = 2048
T = 4096
NCK = 16
HPN = 12
QT = 512
FF = 8192
EXT = (2, 4, 6, 8)
TILES = ((0, 3, 4, 7), (1, 2, 5, 6))
EPS = 1e-5

ENGS = ("tensor", "vector", "scalar", "sync", "gpsimd")


class KB:
    def __init__(self, nc, st):
        self.nc = nc
        self.st = st
        self.ops = {e: [] for e in ENGS}
        self.sems = {}
        self.cnt = {}
        self.waited = {e: {} for e in ENGS}

    def sem(self, name):
        if name not in self.sems:
            self.sems[name] = self.st.enter_context(self.nc.semaphore(name))
            self.cnt[name] = 0
        return name

    def emit(self, eng, builder, waits=(), inc=None, incval=1):
        wl = []
        wmap = self.waited[eng]
        for sname, v in waits:
            if v > 0 and wmap.get(sname, 0) < v:
                wmap[sname] = v
                wl.append((self.sems[sname], v))
        ev = None
        post = None
        if inc is not None:
            self.sem(inc)
            self.cnt[inc] += incval
            ev = (inc, self.cnt[inc])
            post = (self.sems[inc], incval)

        def fn(e, wl=wl, post=post, builder=builder):
            for h, v in wl:
                e.wait_ge(h, v)
            inst = builder(e)
            if post is not None:
                inst.then_inc(post[0], post[1])

        self.ops[eng].append(fn)
        return ev

    def raw_wait(self, eng, waits):
        wl = []
        wmap = self.waited[eng]
        for sname, v in waits:
            if v > 0 and wmap.get(sname, 0) < v:
                wmap[sname] = v
                wl.append((self.sems[sname], v))
        if wl:
            def fn(e, wl=wl):
                for h, v in wl:
                    e.wait_ge(h, v)
            self.ops[eng].append(fn)

    def barrier(self):
        self.sem("bar")
        self.cnt["bar"] += len(ENGS)
        tgt = self.cnt["bar"]
        for e in ENGS:
            def fn(eng, s=self.sems["bar"], tgt=tgt):
                eng.nop().then_inc(s, 1)
                eng.wait_ge(s, tgt)
            self.ops[e].append(fn)
            self.waited[e]["bar"] = tgt


class Buf:
    def __init__(self):
        self.w_ev = {}
        self.r_ev = {}

    @staticmethod
    def _m(d, ev):
        if ev is not None:
            s, v = ev
            if d.get(s, 0) < v:
                d[s] = v

    def wr_waits(self):
        out = dict(self.w_ev)
        for s, v in self.r_ev.items():
            if out.get(s, 0) < v:
                out[s] = v
        return list(out.items())

    def rd_waits(self):
        return list(self.w_ev.items())

    def note_write(self, ev):
        self.w_ev = {}
        self.r_ev = {}
        self._m(self.w_ev, ev)

    def add_write(self, ev):
        self._m(self.w_ev, ev)

    def note_read(self, ev):
        self._m(self.r_ev, ev)


def build_program():
    nc = bass.Bass("TRN2", target_bir_lowering=False, debug=False)
    st = ExitStack()
    kb = KB(nc, st)

    dt = nc.dram_tensor
    xT = dt("xT", [D, T], F32, kind="ExternalInput").ap()
    xTo = dt("xTo", [D, 4 * QT], F32, kind="ExternalInput").ap()
    wq = dt("wq", [D, 1536], F32, kind="ExternalInput").ap()
    wk = dt("wk", [D, 1536], F32, kind="ExternalInput").ap()
    wv = dt("wv", [D, 1536], F32, kind="ExternalInput").ap()
    wo = dt("wo", [1536, D], F32, kind="ExternalInput").ap()
    fk0 = dt("fk0", [1792, 32], F32, kind="ExternalInput").ap()
    fv0 = dt("fv0", [1792, 32], F32, kind="ExternalInput").ap()
    fk1 = dt("fk1", [1024, 64], F32, kind="ExternalInput").ap()
    fv1 = dt("fv1", [1024, 64], F32, kind="ExternalInput").ap()
    bk0 = dt("bk0", [32, 256], F32, kind="ExternalInput").ap()
    bv0 = dt("bv0", [32, 256], F32, kind="ExternalInput").ap()
    bk1 = dt("bk1", [64, 512], F32, kind="ExternalInput").ap()
    bv1 = dt("bv1", [64, 512], F32, kind="ExternalInput").ap()
    wg = dt("wg", [64, 128, NCK * 128], F32, kind="ExternalInput").ap()
    wu = dt("wu", [64, 128, NCK * 128], F32, kind="ExternalInput").ap()
    wd = dt("wd", [FF, D], F32, kind="ExternalInput").ap()
    msk = dt("msk", [8, 128, 2048], BF16, kind="ExternalInput").ap()
    outT = dt("outT", [D, 4 * QT], F32, kind="ExternalOutput").ap()
    kspill = dt("kspill", [HPN, 128, T], BF16, kind="Internal").ap()
    qspill = dt("qspill", [4, HPN, 128, 512], BF16, kind="Internal").ap()
    wospill = dt("wospill", [HPN, 128, D], BF16, kind="Internal").ap()
    hspill = dt("hspill", [64, 128, 2048], BF16, kind="Internal").ap()
    vspill = dt("vspill", [HPN, 128, 32, 130], BF16, kind="Internal").ap()

    E = st.enter_context
    AR_COLS = 92160
    ar = E(nc.sbuf_tensor("ar", [128, AR_COLS], BF16))

    def A(off, cols, r0=0, rows=128):
        return ar[r0:r0 + rows, off:off + cols]

    x1_o = 0
    wkv_o = 0
    xb_o = 49152
    spev_o = 90112
    xos_o = 32768
    qb_o = 40960
    ob_o = 47104
    e_o = 53248
    kst_o = 57344
    vst_o = 65536
    msk_o = 73856
    ring_o = 77952
    r2_o = 32768
    t2_o = 49152
    ring2_o = 57344
    hq_o = 65536
    hsw_o = 69632
    hst_o = 73728

    sm32 = E(nc.sbuf_tensor("sm32", [128, 2048], F32))
    smf = E(nc.sbuf_tensor("smf", [128, 512], F32))
    ckb = E(nc.sbuf_tensor("ckb", [32, 512], BF16))
    ck1b = E(nc.sbuf_tensor("ck1b", [64, 512], BF16))
    cvb = E(nc.sbuf_tensor("cvb", [32, 512], BF16))
    cv1b = E(nc.sbuf_tensor("cv1b", [64, 512], BF16))
    fk0b = E(nc.sbuf_tensor("fk0b", [128, 14 * 32], BF16))
    fv0b = E(nc.sbuf_tensor("fv0b", [128, 14 * 32], BF16))
    fk1b = E(nc.sbuf_tensor("fk1b", [128, 8 * 64], BF16))
    fv1b = E(nc.sbuf_tensor("fv1b", [128, 8 * 64], BF16))
    bk0b = E(nc.sbuf_tensor("bk0b", [32, 256], BF16))
    bv0b = E(nc.sbuf_tensor("bv0b", [32, 256], BF16))
    bk1b = E(nc.sbuf_tensor("bk1b", [64, 512], BF16))
    bv1b = E(nc.sbuf_tensor("bv1b", [64, 512], BF16))
    onesc = E(nc.sbuf_tensor("onesc", [128, 1], BF16))
    onesr = E(nc.sbuf_tensor("onesr", [1, 128], BF16))
    rowb = E(nc.sbuf_tensor("rowb", [1, 1024], BF16))
    ostg = E(nc.sbuf_tensor("ostg", [128, 2 * 512], F32))

    ps = E(nc.psum_tensor("ps", [128, 4096], F32))

    def bank(b, rows=128, cols=512, coff=0, r0=0):
        return ps[r0:r0 + rows, 512 * b + coff: 512 * b + coff + cols]

    for s in ("sp", "sv", "sa", "xb0", "xb1", "xos", "wkv",
              "wr0", "wr1", "wr2", "wr3", "or0", "or1", "wql", "k0", "k1", "v0", "v1", "mk", "ql",
              "sp0", "sp1", "sp2", "sp3", "so0", "so1", "wc0", "wc1", "ws0", "ws1",
              "hs0", "hs1", "hs2", "hs3", "hr0", "hr1", "hr2", "hr3",
              "wd0", "wd1", "wd2", "wd3", "wd4", "wd5", "sg", "hr4", "hr5"):
        kb.sem(s)

    B = dict(
        xb=[Buf(), Buf()], wkv=Buf(),
        ck=Buf(), ck1=Buf(), cv=Buf(), cv1=Buf(),
        spev=[Buf() for _ in range(4)],
        kspill=Buf(), vspill=Buf(), qspill=Buf(), wospill=Buf(),
        xos=Buf(), qb=Buf(), ob=Buf(),
        e=[Buf(), Buf()], e3=[Buf() for _ in range(5)], efin=Buf(),
        kst=[Buf(), Buf()], vst=[Buf(), Buf()], msk=Buf(),
        ring=[Buf() for _ in range(4)],
        x1=Buf(), hspill=Buf(), hq=Buf(), r2=Buf(),
        hsw=[Buf() for _ in range(4)], hst=[Buf() for _ in range(6)],
        wdr=[Buf() for _ in range(6)],
        ostg=[Buf(), Buf()], row=Buf(), sm=Buf(), smf=Buf(),
        bank=[Buf() for _ in range(8)],
        ones=Buf(), onesr=Buf(),
    )

    def dma(eng, sem, out_ap, in_ap, waits=()):
        return kb.emit(eng, lambda e, o=out_ap, i=in_ap: e.dma_start(out=o, in_=i),
                       waits=waits, inc=sem, incval=16)

    ev = kb.emit("vector", lambda e: e.memset(onesc[:, :], 1.0), inc="sv")
    B["ones"].note_write(ev)
    ev = kb.emit("vector", lambda e: e.memset(onesr[:, :], 1.0), inc="sv")
    B["onesr"].note_write(ev)


    cm = lambda a: a.rearrange("(c p) n -> p c n", p=128)
    dma("gpsimd", "wkv", A(wkv_o, NCK * 1536).rearrange("p (c n) -> p c n", c=NCK), cm(wk))
    dma("gpsimd", "wkv", A(wkv_o + NCK * 1536, NCK * 1536).rearrange("p (c n) -> p c n", c=NCK), cm(wv))
    dma("gpsimd", "wkv", fk0b[:, :].rearrange("p (c n) -> p c n", c=14), cm(fk0))
    dma("gpsimd", "wkv", fv0b[:, :].rearrange("p (c n) -> p c n", c=14), cm(fv0))
    dma("gpsimd", "wkv", fk1b[:, :].rearrange("p (c n) -> p c n", c=8), cm(fk1))
    dma("gpsimd", "wkv", fv1b[:, :].rearrange("p (c n) -> p c n", c=8), cm(fv1))
    dma("gpsimd", "wkv", bk0b[:, :], bk0)
    dma("gpsimd", "wkv", bv0b[:, :], bv0)
    dma("gpsimd", "wkv", bk1b[:, :], bk1)
    dma("gpsimd", "wkv", bv1b[:, :], bv1)
    B["wkv"].note_write(("wkv", kb.cnt["wkv"]))

    wkb = A(wkv_o, NCK * 1536)
    wvb = A(wkv_o + NCK * 1536, NCK * 1536)
    wqs_o = 65536
    wqb = A(wqs_o, NCK * 1536)
    Bwq = Buf()
    evq = dma("gpsimd", "wql",
              wqb.rearrange("p (c n) -> p c n", c=NCK), cm(wq))
    Bwq.note_write(evq)
    evs = dma("gpsimd", "wc0",
              wospill.rearrange("h p n -> (h p) n"), wo)
    B["wospill"].add_write(evs)

    spev_n = [0, 0, 0, 0]
    brr = [0]
    evo = kb.emit("vector",
                  lambda e: e.memset(A(spev_o + 1024, 520), 1.0), inc="sv")
    B["spev"][2].note_write(evo)

    def spev_slot(kind):
        if kind == "k":
            sl = 0 if spev_n[0] <= spev_n[1] else 1
            return sl, A(spev_o + sl * 512, 512)
        return 2, A(spev_o + 1024, 520)

    def spill_stage(kind, src_bank_ap, bank_buf, out_hbm, stg_view=None,
                    in_view=None):
        sl, stg = spev_slot(kind)
        sB = B["spev"][sl]
        dst = stg if stg_view is None else stg_view(stg)
        evv = kb.emit("vector",
                      lambda e, dst=dst, src=src_bank_ap: e.tensor_copy(dst, src),
                      waits=bank_buf.rd_waits() + sB.wr_waits(), inc="sv")
        sB.note_write(evv)
        bank_buf.note_read(evv)
        spev_n[sl] += 1
        src = stg if in_view is None else in_view(stg)
        evd = dma("sync", "sp%d" % sl, out_hbm, src, waits=sB.rd_waits())
        sB.note_read(evd)
        return evd

    for tt in range(8):
        par = tt % 2
        xb = A(xb_o + par * NCK * 512, NCK * 512)
        bx = B["xb"][par]
        ev = dma("gpsimd", "xb%d" % par,
                 xb.rearrange("p (c t) -> p c t", c=NCK),
                 cm(xT)[:, :, tt * 512:(tt + 1) * 512],
                 waits=bx.wr_waits())
        bx.note_write(ev)

        def xch(kc, c0=0, w=512, xb=xb):
            return xb[:, kc * 512 + c0: kc * 512 + c0 + w]

        cspec = [("ck", ckb, fk0b, 32, 14, 2), ("cv", cvb, fv0b, 32, 14, 2),
                 ("ck1", ck1b, fk1b, 64, 8, 8), ("cv1", cv1b, fv1b, 64, 8, 8)]
        for bi, (nm, dstb, fb, m, nch, c0) in enumerate(cspec):
            bkB = B["bank"][bi]
            evm = None
            for ci in range(nch):
                w = (bx.rd_waits() + B["wkv"].rd_waits() + bkB.wr_waits()) if ci == 0 else ()
                evm = kb.emit(
                    "tensor",
                    lambda e, bi=bi, m=m, fb=fb, ci=ci, c0=c0, nch=nch, xch=xch: e.matmul(
                        bank(bi, rows=m), fb[:, ci * m:(ci + 1) * m], xch(c0 + ci),
                        start=(ci == 0), stop=(ci == nch - 1)),
                    waits=w, inc=("sp" if ci == nch - 1 else None))
            bkB.note_write(evm)
            bx.note_read(evm)
            B["wkv"].note_read(evm)
            evv = kb.emit("vector",
                          lambda e, bi=bi, m=m, dstb=dstb: e.tensor_copy(
                              dstb[:, :], bank(bi, rows=m)),
                          waits=bkB.rd_waits() + B[nm].wr_waits(), inc="sv")
            B[nm].note_write(evv)
            bkB.note_read(evv)
        brr[0] = 4

        for m in range(HPN):
            b = brr[0] % 8
            brr[0] += 1
            bkB = B["bank"][b]
            if m < 2:
                corr = (bk0b, B["ck"], ckb, m)
            elif m < 6:
                corr = (bk1b, B["ck1"], ck1b, m - 2)
            else:
                corr = None
            nkc = 2 if m < 2 else (8 if m < 6 else NCK)
            evm = None
            for kc in range(nkc):
                w = (bx.rd_waits() + B["wkv"].rd_waits() + bkB.wr_waits()) if kc == 0 else ()
                last = (kc == nkc - 1) and corr is None
                evm = kb.emit(
                    "tensor",
                    lambda e, b=b, m=m, kc=kc, last=last, xch=xch: e.matmul(
                        bank(b), wkb[:, kc * 1536 + m * 128: kc * 1536 + (m + 1) * 128],
                        xch(kc), start=(kc == 0), stop=last),
                    waits=w, inc=("sp" if last else None))
            if corr:
                bb, cB, cbuf, mi = corr
                evm = kb.emit(
                    "tensor",
                    lambda e, b=b, bb=bb, cbuf=cbuf, mi=mi: e.matmul(
                        bank(b), bb[:, mi * 128:(mi + 1) * 128], cbuf[:, :],
                        start=False, stop=True),
                    waits=cB.rd_waits(), inc="sp")
                cB.note_read(evm)
            bkB.note_write(evm)
            bx.note_read(evm)
            B["wkv"].note_read(evm)
            evd = spill_stage("k", bank(b), bkB,
                              kspill[m, :, tt * 512:(tt + 1) * 512])
            B["kspill"].add_write(evd)

        for ts in range(4):
            for dblk in range(3):
                b = brr[0] % 8
                brr[0] += 1
                bkB = B["bank"][b]
                nkc = 8 if dblk == 0 else NCK
                for kc in range(nkc - 1):
                    w = (bx.rd_waits() + B["wkv"].rd_waits() + bkB.wr_waits()) if kc == 0 else ()
                    kb.emit(
                        "tensor",
                        lambda e, b=b, ts=ts, dblk=dblk, kc=kc, xch=xch: e.matmul(
                            bank(b), xch(kc, c0=ts * 128, w=128),
                            wvb[:, kc * 1536 + dblk * 512: kc * 1536 + (dblk + 1) * 512],
                            start=(kc == 0), stop=False),
                        waits=w, inc=None)
                if dblk == 0:
                    kb.emit("tensor",
                            lambda e, b=b, ts=ts: e.matmul(
                                bank(b, cols=256), cvb[:, ts * 128:(ts + 1) * 128],
                                bv0b[:, :], start=False, stop=False),
                            waits=B["cv"].rd_waits(), inc=None)
                    B["cv"].note_read(("sp", kb.cnt["sp"] + 1))
                    kb.emit("tensor",
                            lambda e, b=b, ts=ts: e.matmul(
                                bank(b, cols=256, coff=256),
                                cv1b[:, ts * 128:(ts + 1) * 128],
                                bv1b[:, 0:256], start=False, stop=False),
                            waits=B["cv1"].rd_waits(), inc=None)
                    B["cv1"].note_read(("sp", kb.cnt["sp"] + 1))
                elif dblk == 1:
                    kb.emit("tensor",
                            lambda e, b=b, ts=ts: e.matmul(
                                bank(b, cols=256), cv1b[:, ts * 128:(ts + 1) * 128],
                                bv1b[:, 256:512], start=False, stop=False),
                            waits=B["cv1"].rd_waits(), inc=None)
                    B["cv1"].note_read(("sp", kb.cnt["sp"] + 1))
                evm = kb.emit(
                    "tensor",
                    lambda e, b=b, ts=ts, dblk=dblk, nkc=nkc, xch=xch: e.matmul(
                        bank(b), xch(nkc - 1, c0=ts * 128, w=128),
                        wvb[:, (nkc - 1) * 1536 + dblk * 512: (nkc - 1) * 1536 + (dblk + 1) * 512],
                        start=False, stop=True),
                    waits=(), inc="sp")
                bkB.note_write(evm)
                bx.note_read(evm)
                B["wkv"].note_read(evm)
                c = tt * 4 + ts
                evd = spill_stage(
                    "v", bank(b), bkB,
                    vspill[4 * dblk:4 * dblk + 4, :, c, :].rearrange("h p x -> p h x"),
                    stg_view=lambda stg: stg.rearrange("p (h j x) -> p h j x", h=4, j=2)[:, :, :, 0:64],
                    in_view=lambda stg: stg.rearrange("p (h x) -> p h x", h=4))
                B["vspill"].add_write(evd)

    for s in range(4):
        par = s % 2
        xb = A(xb_o + par * NCK * 512, NCK * 512)
        bx = B["xb"][par]
        ev = dma("gpsimd", "xb%d" % par,
                 xb.rearrange("p (c t) -> p c t", c=NCK),
                 cm(xTo)[:, :, s * 512:(s + 1) * 512],
                 waits=bx.wr_waits())
        bx.note_write(ev)
        for m in range(HPN):
            b = brr[0] % 8
            brr[0] += 1
            bkB = B["bank"][b]
            nkc = 2 if m < 2 else (8 if m < 6 else NCK)
            evm = None
            for kc in range(nkc):
                w = (bx.rd_waits() + Bwq.rd_waits() + bkB.wr_waits()) if kc == 0 else ()
                evm = kb.emit(
                    "tensor",
                    lambda e, b=b, m=m, kc=kc, nkc=nkc, xb=xb: e.matmul(
                        bank(b), wqb[:, kc * 1536 + m * 128: kc * 1536 + (m + 1) * 128],
                        xb[:, kc * 512:(kc + 1) * 512],
                        start=(kc == 0), stop=(kc == nkc - 1)),
                    waits=w, inc=("sp" if kc == nkc - 1 else None))
            bkB.note_write(evm)
            bx.note_read(evm)
            Bwq.note_read(evm)
            evd = spill_stage("k", bank(b), bkB, qspill[s, m])
            B["qspill"].add_write(evd)
    kb.raw_wait("sync", [("sp%d" % i, 16 * spev_n[i]) for i in range(4)])
    kb.raw_wait("gpsimd", [("wkv", kb.cnt["wkv"]), ("xb0", kb.cnt["xb0"]),
                           ("xb1", kb.cnt["xb1"]), ("wql", kb.cnt["wql"]),
                           ("wc0", kb.cnt["wc0"])])
    kb.barrier()

    x1 = A(x1_o, NCK * 2048)
    xos = A(xos_o, NCK * 512)
    qb = A(qb_o, HPN * 512)
    ob = A(ob_o, HPN * 512)

    def ering(i, cols=2048, off=0):
        return A(e_o + i * 2048 + off, cols)

    def ln_rows(kbB):
        pass

    prot = [0]
    erot = [0]
    for s in range(4):
        ext = EXT[s]
        ev = dma("gpsimd", "xos",
                 xos.rearrange("p (c t) -> p c t", c=NCK),
                 cm(xTo)[:, :, s * 512:(s + 1) * 512],
                 waits=B["xos"].wr_waits())
        B["xos"].note_write(ev)

        evql = dma("sync", "ql",
                   qb.rearrange("p (h q) -> p h q", h=HPN),
                   qspill[s].rearrange("h p q -> p h q"),
                   waits=B["qb"].wr_waits() + B["qspill"].rd_waits())
        B["qb"].note_write(evql)

        mw = B["msk"].wr_waits()
        dma("sync", "mk", A(msk_o, 2048), msk[2 * s + 0], waits=mw)
        dma("sync", "mk", A(msk_o + 2048, 2048), msk[2 * s + 1], waits=mw)
        B["msk"].note_write(("mk", kb.cnt["mk"]))

        finq = []
        for hp in range(HPN):
            kpar = hp % 2
            kstg = A(kst_o + kpar * 4096, ext * 512)
            vstg = A(vst_o + kpar * 4160, 130 * 4 * ext)
            kB, vB = B["kst"][kpar], B["vst"][kpar]
            evk = dma("sync", "k%d" % kpar, kstg,
                      kspill[hp, :, 0:ext * 512],
                      waits=kB.wr_waits() + B["kspill"].rd_waits())
            kB.note_write(evk)
            vw = vB.wr_waits() + B["vspill"].rd_waits()
            evv = dma("sync", "v%d" % kpar, A(vst_o + kpar * 4160, 130 * 4 * ext),
                      vspill[hp, :, 0:4 * ext, :], waits=vw)
            vB.note_write(evv)

            avq = []

            def flush_av():
                fn = avq.pop(0)
                fn()

            for j in range(ext):
                for sub in range(4):
                    if 4 * j + sub == 3 and finq:
                        finq.pop(0)()
                    pr = prot[0] % 3
                    prot[0] += 1
                    b0 = 2 * pr
                    w = (kB.rd_waits() + B["qb"].rd_waits()
                         + B["bank"][b0].wr_waits() + B["bank"][b0 + 1].wr_waits())
                    kb.emit("tensor",
                            lambda e, b0=b0, sub=sub, j=j, hp=hp, kstg=kstg: e.matmul(
                                bank(b0),
                                kstg[0:64, j * 512 + sub * 128: j * 512 + (sub + 1) * 128],
                                qb[0:64, hp * 512:(hp + 1) * 512],
                                start=True, stop=True),
                            waits=w, inc="sp")
                    sc_ev = kb.emit(
                        "tensor",
                        lambda e, b0=b0, sub=sub, j=j, hp=hp, kstg=kstg: e.matmul(
                            bank(b0 + 1),
                            kstg[64:128, j * 512 + sub * 128: j * 512 + (sub + 1) * 128],
                            qb[64:128, hp * 512:(hp + 1) * 512],
                            start=True, stop=True, tile_position=(64, 0)),
                        waits=(), inc="sp")
                    kB.note_read(sc_ev)
                    B["qb"].note_read(sc_ev)
                    B["bank"][b0].note_write(sc_ev)
                    B["bank"][b0 + 1].note_write(sc_ev)
                    er = erot[0] % 5
                    erot[0] += 1
                    eB = B["e3"][er]
                    eap = A((e_o + er * 1024) if er < 4 else 88192, 1024)
                    eva = kb.emit("scalar",
                                  lambda e, b0=b0, eap=eap: e.activation(
                                      eap, ps[:, b0 * 512:(b0 + 2) * 512],
                                      AF.Exp, scale=0.125),
                                  waits=[("sp", sc_ev[1])] + eB.wr_waits(), inc="sa")
                    eB.note_write(eva)
                    B["bank"][b0].note_read(eva)
                    B["bank"][b0 + 1].note_read(eva)
                    if j >= ext - 2:
                        midx = j - (ext - 2)
                        for hd in range(2):
                            evmk = kb.emit(
                                "vector",
                                lambda e, eap=eap, midx=midx, sub=sub, hd=hd: e.tensor_tensor(
                                    eap[:, hd * 512:(hd + 1) * 512],
                                    eap[:, hd * 512:(hd + 1) * 512],
                                    A(msk_o + midx * 2048 + sub * 512, 512),
                                    ALU.mult),
                                waits=eB.rd_waits() + list(eB.w_ev.items()) + B["msk"].rd_waits(),
                                inc="sv")
                            eB.add_write(evmk)
                            B["msk"].note_read(evmk)
                    def do_av(c=4 * j + sub, st=(j == 0 and sub == 0),
                              ste=(j == ext - 1 and sub == 3), eB=eB, eap=eap,
                              vstg=vstg, vB=vB):
                        for hd in range(2):
                            w = list(eB.rd_waits()) + vB.rd_waits()
                            if st:
                                w += B["bank"][6 + hd].wr_waits()
                            evav = kb.emit(
                                "tensor",
                                lambda e, c=c, hd=hd, vstg=vstg, eap=eap,
                                       st=st, ste=ste: e.matmul(
                                    bank(6 + hd, rows=65),
                                    vstg[:, 130 * c + 65 * hd: 130 * c + 65 * hd + 65],
                                    eap[:, hd * 512:(hd + 1) * 512],
                                    start=st, stop=ste),
                                waits=w, inc=("sp" if ste else None))
                            if ste:
                                B["bank"][6 + hd].note_write(evav)
                                if hd == 1:
                                    eB.note_read(evav)
                                    vB.note_read(evav)
                            else:
                                nxt = ("sp", kb.cnt["sp"] + 1)
                                eB.note_read(nxt)
                                vB.note_read(nxt)
                    avq.append(do_av)
                    if len(avq) > 4:
                        flush_av()
            while avq:
                flush_av()
            fo = 86144
            fB = B["efin"]
            fw = B["bank"][6].rd_waits() + B["bank"][7].rd_waits()
            ev5 = kb.emit("vector",
                          lambda e, fo=fo: e.tensor_copy(A(fo, 512, rows=65), bank(6, rows=65)),
                          waits=fB.wr_waits() + fw, inc="sv")
            fB.note_write(ev5)
            ev6 = kb.emit("vector",
                          lambda e, fo=fo: e.tensor_copy(A(fo + 512, 512, rows=65), bank(7, rows=65)),
                          waits=(), inc="sv")
            fB.add_write(ev6)
            B["bank"][6].note_read(ev6)
            B["bank"][7].note_read(ev6)
            ev2 = kb.emit("scalar",
                          lambda e, fo=fo: e.activation(sm32[0:1, 0:1024],
                                                 A(fo, 1024, r0=64, rows=1), AF.Ln),
                          waits=fB.rd_waits() + B["sm"].wr_waits(), inc="sa")
            B["sm"].note_write(ev2)
            fB.note_read(ev2)
            ev4 = kb.emit("scalar",
                          lambda e: e.activation(rowb[0:1, 0:1024], sm32[0:1, 0:1024],
                                                 AF.Exp, scale=-1.0),
                          waits=B["sm"].rd_waits() + B["row"].wr_waits(), inc="sa")
            B["row"].note_write(ev4)
            B["sm"].note_read(ev4)
            def fin_b(hp=hp, fo=fo, fB=fB):
                pr = prot[0] % 3
                prot[0] += 1
                bb0 = 2 * pr
                evb = kb.emit("tensor",
                              lambda e, bb0=bb0: e.matmul(bank(bb0), onesr[0:1, :], rowb[0:1, 0:512],
                                                 start=True, stop=True),
                              waits=B["row"].rd_waits() + B["onesr"].rd_waits()
                                    + B["bank"][bb0].wr_waits() + B["bank"][bb0 + 1].wr_waits(),
                              inc="sp")
                evb2 = kb.emit("tensor",
                               lambda e, bb0=bb0: e.matmul(bank(bb0 + 1), onesr[0:1, :], rowb[0:1, 512:1024],
                                                  start=True, stop=True),
                               waits=(), inc="sp")
                B["bank"][bb0].note_write(evb)
                B["bank"][bb0 + 1].note_write(evb2)
                B["row"].note_read(evb2)
                B["onesr"].note_read(evb2)
                evn = kb.emit("vector",
                              lambda e, hp=hp, fo=fo, bb0=bb0: e.tensor_tensor(
                                  ob[0:64, hp * 512:(hp + 1) * 512],
                                  A(fo, 512, rows=64), bank(bb0, rows=64), ALU.mult),
                              waits=fB.rd_waits() + [("sp", evb2[1])]
                                    + B["ob"].wr_waits(),
                              inc="sv")
                evn2 = kb.emit("vector",
                               lambda e, hp=hp, fo=fo, bb0=bb0: e.tensor_tensor(
                                   ob[64:128, hp * 512:(hp + 1) * 512],
                                   A(fo + 512, 512, rows=64), bank(bb0 + 1, rows=64), ALU.mult),
                               waits=(), inc="sv")
                B["ob"].add_write(evn2)
                fB.note_read(evn2)
                B["bank"][bb0].note_read(evn2)
                B["bank"][bb0 + 1].note_read(evn2)
            finq.append(fin_b)
        while finq:
            finq.pop(0)()

        for grp in range(2):
            fms = list(range(8 * grp, 8 * grp + 8))
            nho = 6 if grp == 0 else HPN
            for ho in range(nho):
                rsl = ho % 2
                rB = B["ring"][rsl]
                wst = A(ring_o + rsl * 2048, 2048)
                evl = dma("sync", "or%d" % rsl, wst, wospill[ho],
                          waits=rB.wr_waits() + B["wospill"].rd_waits())
                rB.note_write(evl)
                evx = None
                for i, fm in enumerate(fms):
                    nho_f = 2 if fm < 2 else (6 if fm < 8 else HPN)
                    if ho >= nho_f:
                        continue
                    w = list(rB.rd_waits())
                    if ho == 0:
                        w += B["ob"].rd_waits() + B["bank"][i].wr_waits()
                    evx = kb.emit(
                        "tensor",
                        lambda e, i=i, fm=fm, ho=ho, wst=wst,
                               nho_f=nho_f: e.matmul(
                            bank(i), wst[:, fm * 128:(fm + 1) * 128],
                            ob[:, ho * 512:(ho + 1) * 512],
                            start=(ho == 0), stop=(ho == nho_f - 1)),
                        waits=w, inc="sp")
                    if ho == nho_f - 1:
                        B["bank"][i].note_write(evx)
                        B["ob"].note_read(evx)
                if evx is not None:
                    rB.note_read(evx)
            for i, fm in enumerate(fms):
                evr = kb.emit(
                    "vector",
                    lambda e, i=i, fm=fm: e.tensor_tensor(
                        xos[:, fm * 512:(fm + 1) * 512], bank(i),
                        xos[:, fm * 512:(fm + 1) * 512], ALU.add),
                    waits=B["bank"][i].rd_waits() + B["xos"].wr_waits(), inc="sv")
                B["xos"].add_write(evr)
                B["bank"][i].note_read(evr)

        _emit_ln(kb, B, bank,
                 lambda fm: xos[:, fm * 512:(fm + 1) * 512], B["xos"],
                 lambda fm, s=s: x1[:, fm * 2048 + s * 512: fm * 2048 + s * 512 + 512],
                 ering, sm32, smf, rowb, onesc, onesr, out_f32=None, ostg=None,
                 dma=dma, out_dma=None,
                 tmpA=lambda i: A(86144 + 1024 + i * 512, 512), tmpB=B["efin"])

    kb.raw_wait("sync", [(n, kb.cnt[n]) for n in ("k0", "k1", "v0", "v1", "mk",
                                                   "ql", "or0", "or1")])
    kb.raw_wait("gpsimd", [(n, kb.cnt[n]) for n in ("xos", "wr0", "wr1", "wr2", "wr3")])
    kb.barrier()

    r2 = A(r2_o, NCK * 1024)
    ln2q = []
    for blk in range(2):
        if blk == 1 and ln2q:
            pass
        for q16 in range(16):
            if blk == 1 and q16 == 4 and ln2q:
                ln2q.pop(0)()
            for hq2 in range(2):
                tiles = (q16 + 32 * hq2, q16 + 16 + 32 * hq2)
                for ti, tile in enumerate(tiles):
                    bb = 4 * ((2 * hq2 + ti) % 2)
                    gB, uB = B["bank"][bb], B["bank"][bb + 2]
                    g2B, u2B = B["bank"][bb + 1], B["bank"][bb + 3]
                    rg, ru = 2 * (ti % 2), 2 * (ti % 2) + 1
                    rgB, ruB = B["ring"][rg], B["ring"][ru]
                    wgs = A(ring2_o + rg * 2048, 2048)
                    wus = A(ring2_o + ru * 2048, 2048)
                    evg = dma("gpsimd", "wr%d" % rg, wgs, wg[tile],
                              waits=rgB.wr_waits())
                    rgB.note_write(evg)
                    evu = dma("gpsimd", "wr%d" % ru, wus, wu[tile],
                              waits=ruB.wr_waits())
                    ruB.note_write(evu)
                    for kc in range(NCK):
                        for th in range(2):
                            s = 2 * blk + th
                            w = list(rgB.rd_waits())
                            if kc == 0 and th == 0:
                                w += (B["x1"].rd_waits() + gB.wr_waits()
                                      + g2B.wr_waits() + uB.wr_waits()
                                      + u2B.wr_waits())
                            kb.emit("tensor",
                                    lambda e, bb=bb, th=th, kc=kc, s=s, wgs=wgs: e.matmul(
                                        bank(bb + th),
                                        wgs[:, kc * 128:(kc + 1) * 128],
                                        x1[:, kc * 2048 + s * 512: kc * 2048 + s * 512 + 512],
                                        start=(kc == 0), stop=(kc == NCK - 1)),
                                    waits=w, inc=("sp" if kc == NCK - 1 else None))
                        for th in range(2):
                            s = 2 * blk + th
                            evm = kb.emit(
                                "tensor",
                                lambda e, bb=bb, th=th, kc=kc, s=s, wus=wus: e.matmul(
                                    bank(bb + 2 + th),
                                    wus[:, kc * 128:(kc + 1) * 128],
                                    x1[:, kc * 2048 + s * 512: kc * 2048 + s * 512 + 512],
                                    start=(kc == 0), stop=(kc == NCK - 1)),
                                waits=list(ruB.rd_waits()),
                                inc=("sp" if (kc == NCK - 1 or th == 1) else None))
                        rgB.note_read(evm)
                        ruB.note_read(evm)
                    gB.note_write(("sp", evm[1] - 2))
                    g2B.note_write(("sp", evm[1] - 2))
                    uB.note_write(("sp", evm[1] - 1))
                    u2B.note_write(evm)
                    B["x1"].note_read(evm)
                    tb = A(t2_o, 1024)
                    t2 = A(t2_o + 1024, 1024)
                    at = A(t2_o + 2048, 1024)
                    eB = B["e"][0]
                    gw = [("sp", evm[1] - 2)]
                    evt = kb.emit("scalar",
                                  lambda e, bb=bb, tb=tb: e.activation(
                                      tb, ps[:, bb * 512:(bb + 2) * 512],
                                      AF.Tanh, scale=0.5),
                                  waits=gw + eB.wr_waits(), inc="sa")
                    evt2 = kb.emit("vector",
                                   lambda e, tb=tb, t2=t2: e.tensor_scalar(
                                       t2, tb, 0.5, 0.5, ALU.mult, ALU.add),
                                   waits=[("sa", evt[1])], inc="sv")
                    eva = kb.emit("vector",
                                  lambda e, bb=bb, t2=t2, at=at: e.tensor_tensor(
                                      at, ps[:, bb * 512:(bb + 2) * 512], t2, ALU.mult),
                                  waits=[("sv", evt2[1])] + gw, inc="sv")
                    gB.note_read(eva)
                    g2B.note_read(eva)
                    hqi = ti + 2 * hq2
                    evh = kb.emit("vector",
                                  lambda e, bb=bb, at=at, hqi=hqi: e.tensor_tensor(
                                      A(hq_o + hqi * 1024, 1024), at,
                                      ps[:, (bb + 2) * 512:(bb + 4) * 512], ALU.mult),
                                  waits=[("sv", eva[1]), ("sp", evm[1])]
                                        + B["hq"].wr_waits(), inc="sv")
                    uB.note_read(evh)
                    u2B.note_read(evh)
                    B["hq"].add_write(evh)
                    eB.note_write(evh)
            aoff = [t2_o + 4096 + i * 1024 for i in range(4)]
            n2off = t2_o + 3072
            hql = lambda i: A(hq_o + i * 1024, 1024)
            absl = lambda i: A(aoff[i], 1024)
            wsp = B["hq"].rd_waits() + B["e"][0].wr_waits() + B["e"][1].wr_waits()
            last = None
            for i in range(4):
                last = kb.emit("vector",
                               lambda e, i=i, absl=absl, hql=hql: e.scalar_tensor_tensor(
                                   absl(i), hql(i), -1.0, hql(i), ALU.mult, ALU.max),
                               waits=wsp if i == 0 else [("sv", last[1])], inc="sv")
            m1, m2, n1, n2 = (A(t2_o, 1024), A(t2_o + 1024, 1024),
                              A(t2_o + 2048, 1024), A(n2off, 1024))
            seq = [
                lambda e: e.tensor_tensor(m1, absl(0), absl(1), ALU.max),
                lambda e: e.tensor_tensor(m2, absl(2), absl(3), ALU.max),
                lambda e: e.tensor_tensor(n1, absl(0), absl(1), ALU.min),
                lambda e: e.tensor_tensor(n2, absl(2), absl(3), ALU.min),
                lambda e: e.tensor_tensor(m1, m1, m2, ALU.min),
                lambda e: e.tensor_tensor(n1, n1, n2, ALU.max),
                lambda e: e.tensor_tensor(m1, m1, n1, ALU.max),
            ]
            for f in seq:
                last = kb.emit("vector", f, waits=[("sv", last[1])], inc="sv")
            for i in range(4):
                gekeep = kb.emit("vector",
                                 lambda e, i=i, absl=absl: e.tensor_tensor(
                                     absl(i), absl(i), m1, ALU.is_ge),
                                 waits=[("sv", last[1])], inc="sv")
                tile = q16 + 16 * i
                hwB = B["hsw"][i]
                hsv = kb.emit("vector",
                              lambda e, i=i, absl=absl, hql=hql: e.tensor_tensor(
                                  A(hsw_o + i * 1024, 1024), absl(i), hql(i),
                                  ALU.mult),
                              waits=[("sv", gekeep[1])] + hwB.wr_waits(), inc="sv")
                hwB.note_write(hsv)
                last = hsv
                evd = dma("sync", "hs%d" % i,
                          hspill[tile, :, blk * 1024:(blk + 1) * 1024],
                          A(hsw_o + i * 1024, 1024), waits=hwB.rd_waits())
                hwB.note_read(evd)
                B["hspill"].add_write(evd)
            B["hq"].note_read(last)
            B["e"][0].note_write(last)
            B["e"][1].note_write(last)

        for fq in range(4):
            evx = None
            for ic in range(64):
                rsl = ic % 6
                rB = B["wdr"][rsl]
                wst = A(78848 + rsl * 512, 512)
                evl = dma("gpsimd", "wd%d" % rsl,
                          wst, wd[ic * 128:(ic + 1) * 128, fq * 512:(fq + 1) * 512],
                          waits=rB.wr_waits())
                rB.note_write(evl)
                hsl = ic % 6
                hB = B["hst"][hsl]
                hst = A((hst_o + hsl * 1024) if hsl < 4 else (81920 + (hsl - 4) * 1024), 1024)
                evhl = dma("sync", "hr%d" % hsl, hst,
                           hspill[ic, :, blk * 1024:(blk + 1) * 1024],
                           waits=hB.wr_waits() + B["hspill"].rd_waits())
                hB.note_write(evhl)
                for ft in range(4):
                    for th in range(2):
                        bi = 2 * ft + th
                        w = list(rB.rd_waits()) + hB.rd_waits()
                        if ic == 0:
                            w += B["bank"][bi].wr_waits()
                        evx = kb.emit(
                            "tensor",
                            lambda e, bi=bi, ft=ft, th=th, ic=ic, wst=wst, hst=hst: e.matmul(
                                bank(bi), wst[:, ft * 128:(ft + 1) * 128],
                                hst[:, th * 512:(th + 1) * 512],
                                start=(ic == 0), stop=(ic == 63)),
                            waits=w,
                            inc=("sp" if (ic == 63 or (ft == 3 and th == 1)) else None))
                        if ic == 63:
                            B["bank"][bi].note_write(evx)
                rB.note_read(evx)
                hB.note_read(evx)
                if ic == 63:
                    B["hspill"].note_read(evx)
            for ft in range(4):
                for th in range(2):
                    bi = 2 * ft + th
                    fm = 4 * fq + ft
                    evr = kb.emit(
                        "vector",
                        lambda e, bi=bi, fm=fm, th=th, blk=blk: e.tensor_tensor(
                            r2[:, fm * 1024 + th * 512: fm * 1024 + th * 512 + 512],
                            bank(bi),
                            x1[:, fm * 2048 + (2 * blk + th) * 512:
                               fm * 2048 + (2 * blk + th) * 512 + 512],
                            ALU.add),
                        waits=B["bank"][bi].rd_waits() + B["x1"].rd_waits()
                              + B["r2"].wr_waits(), inc="sv")
                    B["r2"].add_write(evr)
                    B["bank"][bi].note_read(evr)
                    B["x1"].note_read(evr)

        def _ln2(blk=blk):
            for th in range(2):
                s = 2 * blk + th
                _emit_ln(kb, B, bank,
                         lambda fm, th=th: r2[:, fm * 1024 + th * 512:
                                              fm * 1024 + th * 512 + 512],
                         B["r2"], None, ering, sm32, smf, rowb, onesc, onesr,
                         out_f32=True, ostg=ostg, dma=dma,
                         out_dma=lambda fm, s=s: outT[fm * 128:(fm + 1) * 128,
                                                      s * 512:(s + 1) * 512],
                         tmpA=lambda i: A(77824 + i * 512, 512),
                         tmpB=B["e"][1])
        if blk == 1:
            ln2q.append(_ln2)
            for f in ln2q:
                f()
        else:
            ln2q.append(_ln2)

    kb.raw_wait("sync", [("so0", kb.cnt["so0"]), ("so1", kb.cnt["so1"]),
                         ("hs0", kb.cnt["hs0"]), ("hs1", kb.cnt["hs1"]),
                         ("hs2", kb.cnt["hs2"]), ("hs3", kb.cnt["hs3"]),
                         ("hr0", kb.cnt["hr0"]), ("hr1", kb.cnt["hr1"]),
                         ("hr2", kb.cnt["hr2"]), ("hr3", kb.cnt["hr3"])])

    with nc.Block() as block:
        @block.tensor
        def _(t):
            for fn in kb.ops["tensor"]:
                fn(t)

        @block.vector
        def _(v):
            for fn in kb.ops["vector"]:
                fn(v)

        @block.scalar
        def _(sc):
            for fn in kb.ops["scalar"]:
                fn(sc)

        @block.sync
        def _(sy):
            for fn in kb.ops["sync"]:
                fn(sy)

        @block.gpsimd
        def _(g):
            for fn in kb.ops["gpsimd"]:
                fn(g)

    return nc, st


def _emit_ln(kb, B, bank, src_slice, srcB, x1_slice, ering, sm32, smf, rowb,
             onesc, onesr, out_f32, ostg, dma, out_dma, tmpA=None, tmpB=None):
    b0B, b1B = B["bank"][0], B["bank"][1]
    sq = tmpA(0)
    eB = tmpB
    evs2 = None
    for fm in range(16):
        w = list(srcB.rd_waits())
        if fm == 0:
            w += b0B.wr_waits()
        ev1 = kb.emit("tensor",
                      lambda e, fm=fm: e.matmul(
                          bank(0, rows=1), onesc[:, 0:1],
                          src_slice(fm),
                          start=(fm == 0), stop=(fm == 15)),
                      waits=w + B["ones"].rd_waits(),
                      inc=("sp" if fm == 15 else None))
        evq = kb.emit("vector",
                      lambda e, fm=fm, sq=sq: e.tensor_tensor(
                          sq, src_slice(fm),
                          src_slice(fm), ALU.mult),
                      waits=list(srcB.rd_waits()) + eB.wr_waits(), inc="sv")
        eB.note_write(evq)
        w = [("sv", evq[1])]
        if fm == 0:
            w += b1B.wr_waits()
        evs2 = kb.emit("tensor",
                       lambda e, fm=fm, sq=sq: e.matmul(
                           bank(1, rows=1), onesc[:, 0:1], sq,
                           start=(fm == 0), stop=(fm == 15)),
                       waits=w + B["ones"].rd_waits(), inc="sp")
        eB.note_read(evs2)
        if fm == 15:
            b0B.note_write(evs2)
            b1B.note_write(evs2)
            srcB.note_read(evs2)
            B["ones"].note_read(evs2)
    smB = B["sm"]
    ev = kb.emit("vector",
                 lambda e: e.tensor_scalar(sm32[0:1, 0:512], bank(0, rows=1),
                                           1.0 / 2048, None, ALU.mult),
                 waits=b0B.rd_waits() + smB.wr_waits(), inc="sv")
    b0B.note_read(ev)
    ev = kb.emit("vector",
                 lambda e: e.tensor_scalar(sm32[0:1, 512:1024], bank(1, rows=1),
                                           1.0 / 2048, None, ALU.mult),
                 waits=b1B.rd_waits(), inc="sv")
    b1B.note_read(ev)
    ev = kb.emit("vector",
                 lambda e: e.tensor_tensor(sm32[0:1, 1024:1536], sm32[0:1, 0:512],
                                           sm32[0:1, 0:512], ALU.mult),
                 waits=[("sv", ev[1])], inc="sv")
    ev = kb.emit("vector",
                 lambda e: e.tensor_tensor(sm32[0:1, 512:1024], sm32[0:1, 512:1024],
                                           sm32[0:1, 1024:1536], ALU.subtract),
                 waits=[("sv", ev[1])], inc="sv")
    ev = kb.emit("vector",
                 lambda e: e.tensor_scalar(sm32[0:1, 512:1024], sm32[0:1, 512:1024],
                                           EPS, None, ALU.add),
                 waits=[("sv", ev[1])], inc="sv")
    ev = kb.emit("scalar",
                 lambda e: e.activation(sm32[0:1, 1024:1536], sm32[0:1, 512:1024],
                                        AF.Ln),
                 waits=[("sv", ev[1])], inc="sa")
    ev = kb.emit("scalar",
                 lambda e: e.activation(sm32[0:1, 1536:2048], sm32[0:1, 1024:1536],
                                        AF.Exp, scale=-0.5),
                 waits=[("sa", ev[1])], inc="sa")
    ev = kb.emit("vector",
                 lambda e: e.tensor_tensor(sm32[0:1, 1024:1536], sm32[0:1, 0:512],
                                           sm32[0:1, 1536:2048], ALU.mult),
                 waits=[("sa", ev[1])], inc="sv")
    rB = B["row"]
    ev = kb.emit("vector",
                 lambda e: e.tensor_copy(rowb[0:1, 0:512], sm32[0:1, 1536:2048]),
                 waits=[("sv", ev[1])] + rB.wr_waits(), inc="sv")
    ev = kb.emit("vector",
                 lambda e: e.tensor_copy(rowb[0:1, 512:1024], sm32[0:1, 1024:1536]),
                 waits=[("sv", ev[1])], inc="sv")
    rB.note_write(ev)
    smB.note_write(ev)
    b2B, b3B = B["bank"][2], B["bank"][3]
    evb = kb.emit("tensor",
                  lambda e: e.matmul(bank(2), onesr[0:1, :], rowb[0:1, 0:512],
                                     start=True, stop=True),
                  waits=rB.rd_waits() + B["onesr"].rd_waits()
                        + b2B.wr_waits() + b3B.wr_waits(), inc="sp")
    evb2 = kb.emit("tensor",
                   lambda e: e.matmul(bank(3), onesr[0:1, :], rowb[0:1, 512:1024],
                                      start=True, stop=True),
                   waits=(), inc="sp")
    b2B.note_write(evb)
    b3B.note_write(evb2)
    rB.note_read(evb2)
    B["onesr"].note_read(evb2)
    smfB = B["smf"]
    for fm in range(16):
        if out_f32:
            ev1 = kb.emit("vector",
                          lambda e, fm=fm: e.tensor_tensor(
                              smf[:, :], src_slice(fm),
                              bank(2), ALU.mult),
                          waits=list(srcB.rd_waits()) + [("sp", evb2[1])]
                                + smfB.wr_waits(), inc="sv")
            smfB.note_write(ev1)
            so = fm % 2
            oB = B["ostg"][so]
            ev2 = kb.emit("vector",
                          lambda e, fm=fm, so=so: e.tensor_tensor(
                              ostg[:, so * 512:(so + 1) * 512], smf[:, :],
                              bank(3), ALU.subtract),
                          waits=[("sv", ev1[1])] + oB.wr_waits(), inc="sv")
            smfB.note_read(ev2)
            oB.note_write(ev2)
            evd = dma("sync", "so%d" % so, out_dma(fm),
                      ostg[:, so * 512:(so + 1) * 512], waits=oB.rd_waits())
            oB.note_read(evd)
            b2B.note_read(ev2)
            b3B.note_read(ev2)
            srcB.note_read(ev2)
        else:
            tmp = tmpA(1)
            ev1 = kb.emit("vector",
                          lambda e, fm=fm, tmp=tmp: e.tensor_tensor(
                              tmp, src_slice(fm),
                              bank(2), ALU.mult),
                          waits=list(srcB.rd_waits()) + [("sp", evb2[1])]
                                + tmpB.wr_waits(), inc="sv")
            tmpB.note_write(ev1)
            dst = x1_slice(fm)
            ev2 = kb.emit("vector",
                          lambda e, dst=dst, tmp=tmp: e.tensor_tensor(
                              dst, tmp, bank(3), ALU.subtract),
                          waits=[("sv", ev1[1])] + B["x1"].wr_waits(), inc="sv")
            B["x1"].add_write(ev2)
            tmpB.note_read(ev2)
            b2B.note_read(ev2)
            b3B.note_read(ev2)
            srcB.note_read(ev2)



_PERM = None


def _perm():
    global _PERM
    if _PERM is None:
        p = np.arange(FF)
        _PERM = 4 * (p % 2048) + (p // 2048)
    return _PERM


def _wtiles(w):
    r = w.reshape(NCK, 128, 64, 128).transpose(2, 1, 0, 3).reshape(64, 128, NCK * 128)
    return np.ascontiguousarray(r)


def _blockdiag(P):
    nh, r, dk = P.shape
    out = np.zeros((nh * r, nh * dk), np.float32)
    for h in range(nh):
        out[h * r:(h + 1) * r, h * dk:(h + 1) * dk] = P[h]
    return out


def _masks(parity):
    out = np.zeros((8, 128, 2048), np.float32)
    for s in range(4):
        g = TILES[parity][s]
        ext = EXT[s]
        for idx in range(2):
            j = ext - 2 + idx
            for sub in range(4):
                k = 512 * j + 128 * sub + np.arange(128)[:, None]
                q = 512 * g + np.arange(512)[None, :]
                out[2 * s + idx, :, 512 * sub:512 * (sub + 1)] = (k <= q)
    return out.astype(ml_dtypes.bfloat16)


_CACHE = {}


def prep_inputs(L0_attn_x, L0_attn_W_Q, L0_attn_W_K, L0_attn_W_V, L0_attn_W_O,
                L0_attn_FK0, L0_attn_PK0, L0_attn_FV0, L0_attn_PV0,
                L0_attn_FK1, L0_attn_PK1, L0_attn_FV1, L0_attn_PV1,
                L0_ffn_w_gate, L0_ffn_w_up, L0_ffn_w_down):
    x = np.ascontiguousarray(np.asarray(L0_attn_x, np.float32))
    perm = _perm()
    shared = dict(
        wq=np.ascontiguousarray(np.asarray(L0_attn_W_Q, np.float32)),
        wk=np.ascontiguousarray(np.asarray(L0_attn_W_K, np.float32)),
        wv=np.ascontiguousarray(np.asarray(L0_attn_W_V, np.float32)),
        wo=np.ascontiguousarray(np.asarray(L0_attn_W_O, np.float32)),
        fk0=np.ascontiguousarray(np.asarray(L0_attn_FK0, np.float32)),
        fv0=np.ascontiguousarray(np.asarray(L0_attn_FV0, np.float32)),
        fk1=np.ascontiguousarray(np.asarray(L0_attn_FK1, np.float32)),
        fv1=np.ascontiguousarray(np.asarray(L0_attn_FV1, np.float32)),
        bk0=_blockdiag(np.asarray(L0_attn_PK0, np.float32)),
        bv0=_blockdiag(np.asarray(L0_attn_PV0, np.float32)),
        bk1=_blockdiag(np.asarray(L0_attn_PK1, np.float32)),
        bv1=_blockdiag(np.asarray(L0_attn_PV1, np.float32)),
        wg=_wtiles(np.asarray(L0_ffn_w_gate, np.float32)[:, perm]),
        wu=_wtiles(np.asarray(L0_ffn_w_up, np.float32)[:, perm]),
        wd=np.ascontiguousarray(np.asarray(L0_ffn_w_down, np.float32)[perm, :]),
    )
    mks = [_masks(0), _masks(1)]
    in_maps = []
    for c in range(8):
        b, par = c // 2, c % 2
        xb = x[b]
        xTf = np.ascontiguousarray(xb.T)
        own = np.concatenate([xb[512 * g:512 * (g + 1)] for g in TILES[par]], 0)
        in_maps.append(dict(shared, xT=xTf,
                            xTo=np.ascontiguousarray(own.T), msk=mks[par]))
    return in_maps


def kernel(**inputs):
    in_maps = prep_inputs(**inputs)
    if "nc" not in _CACHE:
        nc, st = build_program()
        _CACHE["nc"] = nc
        _CACHE["st"] = st
    res = run_bass_kernel_spmd(_CACHE["nc"], in_maps, core_ids=list(range(8)))
    return assemble(res.results)


def assemble(results):
    out = np.empty((4, T, D), np.float32)
    for c in range(8):
        b, par = c // 2, c % 2
        oT = np.asarray(results[c]["outT"])
        o = np.ascontiguousarray(oT.T)
        for s, g in enumerate(TILES[par]):
            out[b, 512 * g:512 * (g + 1)] = o[512 * s:512 * (s + 1)]
    return out


# version 22
# speedup vs baseline: 1.0656x; 1.0656x over previous
import numpy as np
import ml_dtypes
from contextlib import ExitStack

import concourse.bass as bass
import concourse.mybir as mybir
from concourse.bass_utils import run_bass_kernel_spmd

F32 = mybir.dt.float32
BF16 = mybir.dt.bfloat16
AF = mybir.ActivationFunctionType
ALU = mybir.AluOpType

D = 2048
T = 4096
NCK = 16
HPN = 12
QT = 512
FF = 8192
EXT = (2, 4, 6, 8)
TILES = ((0, 3, 4, 7), (1, 2, 5, 6))
EPS = 1e-5

ENGS = ("tensor", "vector", "scalar", "sync", "gpsimd")


class KB:
    def __init__(self, nc, st):
        self.nc = nc
        self.st = st
        self.ops = {e: [] for e in ENGS}
        self.sems = {}
        self.cnt = {}
        self.waited = {e: {} for e in ENGS}

    def sem(self, name):
        if name not in self.sems:
            self.sems[name] = self.st.enter_context(self.nc.semaphore(name))
            self.cnt[name] = 0
        return name

    def emit(self, eng, builder, waits=(), inc=None, incval=1):
        wl = []
        wmap = self.waited[eng]
        for sname, v in waits:
            if v > 0 and wmap.get(sname, 0) < v:
                wmap[sname] = v
                wl.append((self.sems[sname], v))
        ev = None
        post = None
        if inc is not None:
            self.sem(inc)
            self.cnt[inc] += incval
            ev = (inc, self.cnt[inc])
            post = (self.sems[inc], incval)

        def fn(e, wl=wl, post=post, builder=builder):
            for h, v in wl:
                e.wait_ge(h, v)
            inst = builder(e)
            if post is not None:
                inst.then_inc(post[0], post[1])

        self.ops[eng].append(fn)
        return ev

    def raw_wait(self, eng, waits):
        wl = []
        wmap = self.waited[eng]
        for sname, v in waits:
            if v > 0 and wmap.get(sname, 0) < v:
                wmap[sname] = v
                wl.append((self.sems[sname], v))
        if wl:
            def fn(e, wl=wl):
                for h, v in wl:
                    e.wait_ge(h, v)
            self.ops[eng].append(fn)

    def barrier(self):
        self.sem("bar")
        self.cnt["bar"] += len(ENGS)
        tgt = self.cnt["bar"]
        for e in ENGS:
            def fn(eng, s=self.sems["bar"], tgt=tgt):
                eng.nop().then_inc(s, 1)
                eng.wait_ge(s, tgt)
            self.ops[e].append(fn)
            self.waited[e]["bar"] = tgt


class Buf:
    def __init__(self):
        self.w_ev = {}
        self.r_ev = {}

    @staticmethod
    def _m(d, ev):
        if ev is not None:
            s, v = ev
            if d.get(s, 0) < v:
                d[s] = v

    def wr_waits(self):
        out = dict(self.w_ev)
        for s, v in self.r_ev.items():
            if out.get(s, 0) < v:
                out[s] = v
        return list(out.items())

    def rd_waits(self):
        return list(self.w_ev.items())

    def note_write(self, ev):
        self.w_ev = {}
        self.r_ev = {}
        self._m(self.w_ev, ev)

    def add_write(self, ev):
        self._m(self.w_ev, ev)

    def note_read(self, ev):
        self._m(self.r_ev, ev)


def build_program():
    nc = bass.Bass("TRN2", target_bir_lowering=False, debug=False)
    st = ExitStack()
    kb = KB(nc, st)

    dt = nc.dram_tensor
    xT = dt("xT", [D, T], F32, kind="ExternalInput").ap()
    xTo = dt("xTo", [D, 4 * QT], F32, kind="ExternalInput").ap()
    wq = dt("wq", [D, 1536], F32, kind="ExternalInput").ap()
    wk = dt("wk", [D, 1536], F32, kind="ExternalInput").ap()
    wv = dt("wv", [D, 1536], F32, kind="ExternalInput").ap()
    wo = dt("wo", [1536, D], F32, kind="ExternalInput").ap()
    fk0 = dt("fk0", [1792, 32], F32, kind="ExternalInput").ap()
    fv0 = dt("fv0", [1792, 32], F32, kind="ExternalInput").ap()
    fk1 = dt("fk1", [1024, 64], F32, kind="ExternalInput").ap()
    fv1 = dt("fv1", [1024, 64], F32, kind="ExternalInput").ap()
    bk0 = dt("bk0", [32, 256], F32, kind="ExternalInput").ap()
    bv0 = dt("bv0", [32, 256], F32, kind="ExternalInput").ap()
    bk1 = dt("bk1", [64, 512], F32, kind="ExternalInput").ap()
    bv1 = dt("bv1", [64, 512], F32, kind="ExternalInput").ap()
    wg = dt("wg", [64, 128, NCK * 128], F32, kind="ExternalInput").ap()
    wu = dt("wu", [64, 128, NCK * 128], F32, kind="ExternalInput").ap()
    wd = dt("wd", [FF, D], F32, kind="ExternalInput").ap()
    msk = dt("msk", [8, 128, 2048], BF16, kind="ExternalInput").ap()
    outT = dt("outT", [D, 4 * QT], F32, kind="ExternalOutput").ap()
    kspill = dt("kspill", [HPN, 128, T], BF16, kind="Internal").ap()
    qspill = dt("qspill", [4, HPN, 128, 512], BF16, kind="Internal").ap()
    wospill = dt("wospill", [HPN, 128, D], BF16, kind="Internal").ap()
    hspill = dt("hspill", [64, 128, 2048], BF16, kind="Internal").ap()
    vspill = dt("vspill", [HPN, 128, 32, 130], BF16, kind="Internal").ap()

    E = st.enter_context
    AR_COLS = 92160
    ar = E(nc.sbuf_tensor("ar", [128, AR_COLS], BF16))

    def A(off, cols, r0=0, rows=128):
        return ar[r0:r0 + rows, off:off + cols]

    x1_o = 0
    wkv_o = 0
    xb_o = 49152
    spev_o = 90112
    xos_o = 32768
    qb_o = 40960
    ob_o = 47104
    e_o = 53248
    kst_o = 57344
    vst_o = 65536
    msk_o = 73856
    ring_o = 77952
    r2_o = 32768
    t2_o = 49152
    ring2_o = 57344
    hq_o = 65536
    hsw_o = 69632
    hst_o = 73728

    sm32 = E(nc.sbuf_tensor("sm32", [128, 2048], F32))
    smf = E(nc.sbuf_tensor("smf", [128, 512], F32))
    ckb = E(nc.sbuf_tensor("ckb", [32, 512], BF16))
    ck1b = E(nc.sbuf_tensor("ck1b", [64, 512], BF16))
    cvb = E(nc.sbuf_tensor("cvb", [32, 512], BF16))
    cv1b = E(nc.sbuf_tensor("cv1b", [64, 512], BF16))
    fk0b = E(nc.sbuf_tensor("fk0b", [128, 14 * 32], BF16))
    fv0b = E(nc.sbuf_tensor("fv0b", [128, 14 * 32], BF16))
    fk1b = E(nc.sbuf_tensor("fk1b", [128, 8 * 64], BF16))
    fv1b = E(nc.sbuf_tensor("fv1b", [128, 8 * 64], BF16))
    bk0b = E(nc.sbuf_tensor("bk0b", [32, 256], BF16))
    bv0b = E(nc.sbuf_tensor("bv0b", [32, 256], BF16))
    bk1b = E(nc.sbuf_tensor("bk1b", [64, 512], BF16))
    bv1b = E(nc.sbuf_tensor("bv1b", [64, 512], BF16))
    onesc = E(nc.sbuf_tensor("onesc", [128, 1], BF16))
    onesr = E(nc.sbuf_tensor("onesr", [1, 128], BF16))
    rowb = E(nc.sbuf_tensor("rowb", [1, 1024], BF16))
    ostg = E(nc.sbuf_tensor("ostg", [128, 2 * 512], F32))

    ps = E(nc.psum_tensor("ps", [128, 4096], F32))

    def bank(b, rows=128, cols=512, coff=0, r0=0):
        return ps[r0:r0 + rows, 512 * b + coff: 512 * b + coff + cols]

    for s in ("sp", "sv", "sa", "xb0", "xb1", "xos", "wkv",
              "wr0", "wr1", "wr2", "wr3", "or0", "or1", "wql", "k0", "k1", "v0", "v1", "mk", "ql",
              "sp0", "sp1", "sp2", "sp3", "so0", "so1", "wc0", "wc1", "ws0", "ws1",
              "hs0", "hs1", "hs2", "hs3", "hr0", "hr1", "hr2", "hr3",
              "wd0", "wd1", "wd2", "wd3", "wd4", "wd5", "sg", "hr4", "hr5"):
        kb.sem(s)

    B = dict(
        xb=[Buf(), Buf()], wkv=Buf(),
        ck=Buf(), ck1=Buf(), cv=Buf(), cv1=Buf(),
        spev=[Buf() for _ in range(4)],
        kspill=Buf(), vspill=Buf(), qspill=Buf(), wospill=Buf(),
        xos=Buf(), qb=Buf(), ob=Buf(),
        e=[Buf(), Buf()], e3=[Buf() for _ in range(5)], efin=Buf(),
        kst=[Buf(), Buf()], vst=[Buf(), Buf()], msk=Buf(),
        ring=[Buf() for _ in range(4)],
        x1=Buf(), hspill=Buf(), hq=Buf(), r2=Buf(),
        hsw=[Buf() for _ in range(4)], hst=[Buf() for _ in range(6)],
        wdr=[Buf() for _ in range(6)],
        ostg=[Buf(), Buf()], row=Buf(), sm=Buf(), smf=Buf(),
        bank=[Buf() for _ in range(8)],
        ones=Buf(), onesr=Buf(),
    )

    def dma(eng, sem, out_ap, in_ap, waits=()):
        return kb.emit(eng, lambda e, o=out_ap, i=in_ap: e.dma_start(out=o, in_=i),
                       waits=waits, inc=sem, incval=16)

    ev = kb.emit("vector", lambda e: e.memset(onesc[:, :], 1.0), inc="sv")
    B["ones"].note_write(ev)
    ev = kb.emit("vector", lambda e: e.memset(onesr[:, :], 1.0), inc="sv")
    B["onesr"].note_write(ev)


    cm = lambda a: a.rearrange("(c p) n -> p c n", p=128)
    dma("gpsimd", "wkv", A(wkv_o, NCK * 1536).rearrange("p (c n) -> p c n", c=NCK), cm(wk))
    dma("gpsimd", "wkv", A(wkv_o + NCK * 1536, NCK * 1536).rearrange("p (c n) -> p c n", c=NCK), cm(wv))
    dma("gpsimd", "wkv", fk0b[:, :].rearrange("p (c n) -> p c n", c=14), cm(fk0))
    dma("gpsimd", "wkv", fv0b[:, :].rearrange("p (c n) -> p c n", c=14), cm(fv0))
    dma("gpsimd", "wkv", fk1b[:, :].rearrange("p (c n) -> p c n", c=8), cm(fk1))
    dma("gpsimd", "wkv", fv1b[:, :].rearrange("p (c n) -> p c n", c=8), cm(fv1))
    dma("gpsimd", "wkv", bk0b[:, :], bk0)
    dma("gpsimd", "wkv", bv0b[:, :], bv0)
    dma("gpsimd", "wkv", bk1b[:, :], bk1)
    dma("gpsimd", "wkv", bv1b[:, :], bv1)
    B["wkv"].note_write(("wkv", kb.cnt["wkv"]))

    wkb = A(wkv_o, NCK * 1536)
    wvb = A(wkv_o + NCK * 1536, NCK * 1536)
    wqs_o = 65536
    wqb = A(wqs_o, NCK * 1536)
    Bwq = Buf()
    evq = dma("gpsimd", "wql",
              wqb.rearrange("p (c n) -> p c n", c=NCK), cm(wq))
    Bwq.note_write(evq)
    evs = dma("gpsimd", "wc0",
              wospill.rearrange("h p n -> (h p) n"), wo)
    B["wospill"].add_write(evs)

    spev_n = [0, 0, 0, 0]
    brr = [0]
    evo = kb.emit("vector",
                  lambda e: e.memset(A(spev_o + 1024, 520), 1.0), inc="sv")
    B["spev"][2].note_write(evo)

    def spev_slot(kind):
        if kind == "k":
            sl = 0 if spev_n[0] <= spev_n[1] else 1
            return sl, A(spev_o + sl * 512, 512)
        return 2, A(spev_o + 1024, 520)

    def spill_stage(kind, src_bank_ap, bank_buf, out_hbm, stg_view=None,
                    in_view=None):
        sl, stg = spev_slot(kind)
        sB = B["spev"][sl]
        dst = stg if stg_view is None else stg_view(stg)
        evv = kb.emit("vector",
                      lambda e, dst=dst, src=src_bank_ap: e.tensor_copy(dst, src),
                      waits=bank_buf.rd_waits() + sB.wr_waits(), inc="sv")
        sB.note_write(evv)
        bank_buf.note_read(evv)
        spev_n[sl] += 1
        src = stg if in_view is None else in_view(stg)
        evd = dma("sync", "sp%d" % sl, out_hbm, src, waits=sB.rd_waits())
        sB.note_read(evd)
        return evd

    for tt in range(8):
        par = tt % 2
        xb = A(xb_o + par * NCK * 512, NCK * 512)
        bx = B["xb"][par]
        ev = dma("gpsimd", "xb%d" % par,
                 xb.rearrange("p (c t) -> p c t", c=NCK),
                 cm(xT)[:, :, tt * 512:(tt + 1) * 512],
                 waits=bx.wr_waits())
        bx.note_write(ev)

        def xch(kc, c0=0, w=512, xb=xb):
            return xb[:, kc * 512 + c0: kc * 512 + c0 + w]

        cspec = [("ck", ckb, fk0b, 32, 14, 2), ("cv", cvb, fv0b, 32, 14, 2),
                 ("ck1", ck1b, fk1b, 64, 8, 8), ("cv1", cv1b, fv1b, 64, 8, 8)]
        for bi, (nm, dstb, fb, m, nch, c0) in enumerate(cspec):
            bkB = B["bank"][bi]
            evm = None
            for ci in range(nch):
                w = (bx.rd_waits() + B["wkv"].rd_waits() + bkB.wr_waits()) if ci == 0 else ()
                evm = kb.emit(
                    "tensor",
                    lambda e, bi=bi, m=m, fb=fb, ci=ci, c0=c0, nch=nch, xch=xch: e.matmul(
                        bank(bi, rows=m), fb[:, ci * m:(ci + 1) * m], xch(c0 + ci),
                        start=(ci == 0), stop=(ci == nch - 1)),
                    waits=w, inc=("sp" if ci == nch - 1 else None))
            bkB.note_write(evm)
            bx.note_read(evm)
            B["wkv"].note_read(evm)
            evv = kb.emit("vector",
                          lambda e, bi=bi, m=m, dstb=dstb: e.tensor_copy(
                              dstb[:, :], bank(bi, rows=m)),
                          waits=bkB.rd_waits() + B[nm].wr_waits(), inc="sv")
            B[nm].note_write(evv)
            bkB.note_read(evv)
        brr[0] = 4

        for m in range(HPN):
            b = brr[0] % 8
            brr[0] += 1
            bkB = B["bank"][b]
            if m < 2:
                corr = (bk0b, B["ck"], ckb, m)
            elif m < 6:
                corr = (bk1b, B["ck1"], ck1b, m - 2)
            else:
                corr = None
            nkc = 2 if m < 2 else (8 if m < 6 else NCK)
            evm = None
            for kc in range(nkc):
                w = (bx.rd_waits() + B["wkv"].rd_waits() + bkB.wr_waits()) if kc == 0 else ()
                last = (kc == nkc - 1) and corr is None
                evm = kb.emit(
                    "tensor",
                    lambda e, b=b, m=m, kc=kc, last=last, xch=xch: e.matmul(
                        bank(b), wkb[:, kc * 1536 + m * 128: kc * 1536 + (m + 1) * 128],
                        xch(kc), start=(kc == 0), stop=last),
                    waits=w, inc=("sp" if last else None))
            if corr:
                bb, cB, cbuf, mi = corr
                evm = kb.emit(
                    "tensor",
                    lambda e, b=b, bb=bb, cbuf=cbuf, mi=mi: e.matmul(
                        bank(b), bb[:, mi * 128:(mi + 1) * 128], cbuf[:, :],
                        start=False, stop=True),
                    waits=cB.rd_waits(), inc="sp")
                cB.note_read(evm)
            bkB.note_write(evm)
            bx.note_read(evm)
            B["wkv"].note_read(evm)
            evd = spill_stage("k", bank(b), bkB,
                              kspill[m, :, tt * 512:(tt + 1) * 512])
            B["kspill"].add_write(evd)

        for ts in range(4):
            for dblk in range(3):
                b = brr[0] % 8
                brr[0] += 1
                bkB = B["bank"][b]
                nkc = 8 if dblk == 0 else NCK
                for kc in range(nkc - 1):
                    w = (bx.rd_waits() + B["wkv"].rd_waits() + bkB.wr_waits()) if kc == 0 else ()
                    kb.emit(
                        "tensor",
                        lambda e, b=b, ts=ts, dblk=dblk, kc=kc, xch=xch: e.matmul(
                            bank(b), xch(kc, c0=ts * 128, w=128),
                            wvb[:, kc * 1536 + dblk * 512: kc * 1536 + (dblk + 1) * 512],
                            start=(kc == 0), stop=False),
                        waits=w, inc=None)
                if dblk == 0:
                    kb.emit("tensor",
                            lambda e, b=b, ts=ts: e.matmul(
                                bank(b, cols=256), cvb[:, ts * 128:(ts + 1) * 128],
                                bv0b[:, :], start=False, stop=False),
                            waits=B["cv"].rd_waits(), inc=None)
                    B["cv"].note_read(("sp", kb.cnt["sp"] + 1))
                    kb.emit("tensor",
                            lambda e, b=b, ts=ts: e.matmul(
                                bank(b, cols=256, coff=256),
                                cv1b[:, ts * 128:(ts + 1) * 128],
                                bv1b[:, 0:256], start=False, stop=False),
                            waits=B["cv1"].rd_waits(), inc=None)
                    B["cv1"].note_read(("sp", kb.cnt["sp"] + 1))
                elif dblk == 1:
                    kb.emit("tensor",
                            lambda e, b=b, ts=ts: e.matmul(
                                bank(b, cols=256), cv1b[:, ts * 128:(ts + 1) * 128],
                                bv1b[:, 256:512], start=False, stop=False),
                            waits=B["cv1"].rd_waits(), inc=None)
                    B["cv1"].note_read(("sp", kb.cnt["sp"] + 1))
                evm = kb.emit(
                    "tensor",
                    lambda e, b=b, ts=ts, dblk=dblk, nkc=nkc, xch=xch: e.matmul(
                        bank(b), xch(nkc - 1, c0=ts * 128, w=128),
                        wvb[:, (nkc - 1) * 1536 + dblk * 512: (nkc - 1) * 1536 + (dblk + 1) * 512],
                        start=False, stop=True),
                    waits=(), inc="sp")
                bkB.note_write(evm)
                bx.note_read(evm)
                B["wkv"].note_read(evm)
                c = tt * 4 + ts
                evd = spill_stage(
                    "v", bank(b), bkB,
                    vspill[4 * dblk:4 * dblk + 4, :, c, :].rearrange("h p x -> p h x"),
                    stg_view=lambda stg: stg.rearrange("p (h j x) -> p h j x", h=4, j=2)[:, :, :, 0:64],
                    in_view=lambda stg: stg.rearrange("p (h x) -> p h x", h=4))
                B["vspill"].add_write(evd)

    for s in range(4):
        par = s % 2
        xb = A(xb_o + par * NCK * 512, NCK * 512)
        bx = B["xb"][par]
        ev = dma("gpsimd", "xb%d" % par,
                 xb.rearrange("p (c t) -> p c t", c=NCK),
                 cm(xTo)[:, :, s * 512:(s + 1) * 512],
                 waits=bx.wr_waits())
        bx.note_write(ev)
        for m in range(HPN):
            b = brr[0] % 8
            brr[0] += 1
            bkB = B["bank"][b]
            nkc = 2 if m < 2 else (8 if m < 6 else NCK)
            evm = None
            for kc in range(nkc):
                w = (bx.rd_waits() + Bwq.rd_waits() + bkB.wr_waits()) if kc == 0 else ()
                evm = kb.emit(
                    "tensor",
                    lambda e, b=b, m=m, kc=kc, nkc=nkc, xb=xb: e.matmul(
                        bank(b), wqb[:, kc * 1536 + m * 128: kc * 1536 + (m + 1) * 128],
                        xb[:, kc * 512:(kc + 1) * 512],
                        start=(kc == 0), stop=(kc == nkc - 1)),
                    waits=w, inc=("sp" if kc == nkc - 1 else None))
            bkB.note_write(evm)
            bx.note_read(evm)
            Bwq.note_read(evm)
            evd = spill_stage("k", bank(b), bkB, qspill[s, m])
            B["qspill"].add_write(evd)
    kb.raw_wait("sync", [("sp%d" % i, 16 * spev_n[i]) for i in range(4)])
    kb.raw_wait("gpsimd", [("wkv", kb.cnt["wkv"]), ("xb0", kb.cnt["xb0"]),
                           ("xb1", kb.cnt["xb1"]), ("wql", kb.cnt["wql"]),
                           ("wc0", kb.cnt["wc0"])])
    kb.barrier()

    x1 = A(x1_o, NCK * 2048)
    xos = A(xos_o, NCK * 512)
    qb = A(qb_o, HPN * 512)
    ob = A(ob_o, HPN * 512)

    def ering(i, cols=2048, off=0):
        return A(e_o + i * 2048 + off, cols)

    def ln_rows(kbB):
        pass

    prot = [0]
    erot = [0]
    for s in range(4):
        ext = EXT[s]
        ev = dma("gpsimd", "xos",
                 xos.rearrange("p (c t) -> p c t", c=NCK),
                 cm(xTo)[:, :, s * 512:(s + 1) * 512],
                 waits=B["xos"].wr_waits())
        B["xos"].note_write(ev)

        evql = dma("sync", "ql",
                   qb.rearrange("p (h q) -> p h q", h=HPN),
                   qspill[s].rearrange("h p q -> p h q"),
                   waits=B["qb"].wr_waits() + B["qspill"].rd_waits())
        B["qb"].note_write(evql)

        mw = B["msk"].wr_waits()
        dma("sync", "mk", A(msk_o, 2048), msk[2 * s + 0], waits=mw)
        dma("sync", "mk", A(msk_o + 2048, 2048), msk[2 * s + 1], waits=mw)
        B["msk"].note_write(("mk", kb.cnt["mk"]))

        finq = []
        for hp in range(HPN):
            kpar = hp % 2
            kstg = A(kst_o + kpar * 4096, ext * 512)
            vstg = A(vst_o + kpar * 4160, 130 * 4 * ext)
            kB, vB = B["kst"][kpar], B["vst"][kpar]
            evk = dma("sync", "k%d" % kpar, kstg,
                      kspill[hp, :, 0:ext * 512],
                      waits=kB.wr_waits() + B["kspill"].rd_waits())
            kB.note_write(evk)
            vw = vB.wr_waits() + B["vspill"].rd_waits()
            evv = dma("sync", "v%d" % kpar, A(vst_o + kpar * 4160, 130 * 4 * ext),
                      vspill[hp, :, 0:4 * ext, :], waits=vw)
            vB.note_write(evv)

            avq = []

            def flush_av():
                fn = avq.pop(0)
                fn()

            for j in range(ext):
                for sub in range(4):
                    if 4 * j + sub == 2 and finq:
                        finq[0][0]()
                    if 4 * j + sub == 6 and finq:
                        finq.pop(0)[1]()
                    pr = prot[0] % 3
                    prot[0] += 1
                    b0 = 2 * pr
                    w = (kB.rd_waits() + B["qb"].rd_waits()
                         + B["bank"][b0].wr_waits() + B["bank"][b0 + 1].wr_waits())
                    kb.emit("tensor",
                            lambda e, b0=b0, sub=sub, j=j, hp=hp, kstg=kstg: e.matmul(
                                bank(b0),
                                kstg[0:64, j * 512 + sub * 128: j * 512 + (sub + 1) * 128],
                                qb[0:64, hp * 512:(hp + 1) * 512],
                                start=True, stop=True),
                            waits=w, inc="sp")
                    sc_ev = kb.emit(
                        "tensor",
                        lambda e, b0=b0, sub=sub, j=j, hp=hp, kstg=kstg: e.matmul(
                            bank(b0 + 1),
                            kstg[64:128, j * 512 + sub * 128: j * 512 + (sub + 1) * 128],
                            qb[64:128, hp * 512:(hp + 1) * 512],
                            start=True, stop=True, tile_position=(64, 0)),
                        waits=(), inc="sp")
                    kB.note_read(sc_ev)
                    B["qb"].note_read(sc_ev)
                    B["bank"][b0].note_write(sc_ev)
                    B["bank"][b0 + 1].note_write(sc_ev)
                    er = erot[0] % 5
                    erot[0] += 1
                    eB = B["e3"][er]
                    eap = A((e_o + er * 1024) if er < 4 else 88192, 1024)
                    eva = kb.emit("scalar",
                                  lambda e, b0=b0, eap=eap: e.activation(
                                      eap, ps[:, b0 * 512:(b0 + 2) * 512],
                                      AF.Exp, scale=0.125),
                                  waits=[("sp", sc_ev[1])] + eB.wr_waits(), inc="sa")
                    eB.note_write(eva)
                    B["bank"][b0].note_read(eva)
                    B["bank"][b0 + 1].note_read(eva)
                    if j >= ext - 2:
                        midx = j - (ext - 2)
                        for hd in range(2):
                            evmk = kb.emit(
                                "vector",
                                lambda e, eap=eap, midx=midx, sub=sub, hd=hd: e.tensor_tensor(
                                    eap[:, hd * 512:(hd + 1) * 512],
                                    eap[:, hd * 512:(hd + 1) * 512],
                                    A(msk_o + midx * 2048 + sub * 512, 512),
                                    ALU.mult),
                                waits=eB.rd_waits() + list(eB.w_ev.items()) + B["msk"].rd_waits(),
                                inc="sv")
                            eB.add_write(evmk)
                            B["msk"].note_read(evmk)
                    def do_av(c=4 * j + sub, st=(j == 0 and sub == 0),
                              ste=(j == ext - 1 and sub == 3), eB=eB, eap=eap,
                              vstg=vstg, vB=vB):
                        for hd in range(2):
                            w = list(eB.rd_waits()) + vB.rd_waits()
                            if st:
                                w += B["bank"][6 + hd].wr_waits()
                            evav = kb.emit(
                                "tensor",
                                lambda e, c=c, hd=hd, vstg=vstg, eap=eap,
                                       st=st, ste=ste: e.matmul(
                                    bank(6 + hd, rows=65),
                                    vstg[:, 130 * c + 65 * hd: 130 * c + 65 * hd + 65],
                                    eap[:, hd * 512:(hd + 1) * 512],
                                    start=st, stop=ste),
                                waits=w, inc=("sp" if ste else None))
                            if ste:
                                B["bank"][6 + hd].note_write(evav)
                                if hd == 1:
                                    eB.note_read(evav)
                                    vB.note_read(evav)
                            else:
                                nxt = ("sp", kb.cnt["sp"] + 1)
                                eB.note_read(nxt)
                                vB.note_read(nxt)
                    avq.append(do_av)
                    if len(avq) > 4:
                        flush_av()
            while avq:
                flush_av()
            fo = 86144
            fB = B["efin"]
            fw = B["bank"][6].rd_waits() + B["bank"][7].rd_waits()
            ev5 = kb.emit("vector",
                          lambda e, fo=fo: e.tensor_copy(A(fo, 512, rows=65), bank(6, rows=65)),
                          waits=fB.wr_waits() + fw, inc="sv")
            fB.note_write(ev5)
            ev6 = kb.emit("vector",
                          lambda e, fo=fo: e.tensor_copy(A(fo + 512, 512, rows=65), bank(7, rows=65)),
                          waits=(), inc="sv")
            fB.add_write(ev6)
            B["bank"][6].note_read(ev6)
            B["bank"][7].note_read(ev6)
            def fin_a(fo=fo, fB=fB):
                ev2 = kb.emit("scalar",
                              lambda e, fo=fo: e.activation(sm32[0:1, 0:1024],
                                                     A(fo, 1024, r0=64, rows=1), AF.Ln),
                              waits=fB.rd_waits() + B["sm"].wr_waits(), inc="sa")
                B["sm"].note_write(ev2)
                fB.note_read(ev2)
                ev4 = kb.emit("scalar",
                              lambda e: e.activation(rowb[0:1, 0:1024], sm32[0:1, 0:1024],
                                                     AF.Exp, scale=-1.0),
                              waits=B["sm"].rd_waits() + B["row"].wr_waits(), inc="sa")
                B["row"].note_write(ev4)
                B["sm"].note_read(ev4)
            def fin_b(hp=hp, fo=fo, fB=fB):
                pr = prot[0] % 3
                prot[0] += 1
                bb0 = 2 * pr
                evb = kb.emit("tensor",
                              lambda e, bb0=bb0: e.matmul(bank(bb0), onesr[0:1, :], rowb[0:1, 0:512],
                                                 start=True, stop=True),
                              waits=B["row"].rd_waits() + B["onesr"].rd_waits()
                                    + B["bank"][bb0].wr_waits() + B["bank"][bb0 + 1].wr_waits(),
                              inc="sp")
                evb2 = kb.emit("tensor",
                               lambda e, bb0=bb0: e.matmul(bank(bb0 + 1), onesr[0:1, :], rowb[0:1, 512:1024],
                                                  start=True, stop=True),
                               waits=(), inc="sp")
                B["bank"][bb0].note_write(evb)
                B["bank"][bb0 + 1].note_write(evb2)
                B["row"].note_read(evb2)
                B["onesr"].note_read(evb2)
                evn = kb.emit("vector",
                              lambda e, hp=hp, fo=fo, bb0=bb0: e.tensor_tensor(
                                  ob[0:64, hp * 512:(hp + 1) * 512],
                                  A(fo, 512, rows=64), bank(bb0, rows=64), ALU.mult),
                              waits=fB.rd_waits() + [("sp", evb2[1])]
                                    + B["ob"].wr_waits(),
                              inc="sv")
                evn2 = kb.emit("vector",
                               lambda e, hp=hp, fo=fo, bb0=bb0: e.tensor_tensor(
                                   ob[64:128, hp * 512:(hp + 1) * 512],
                                   A(fo + 512, 512, rows=64), bank(bb0 + 1, rows=64), ALU.mult),
                               waits=(), inc="sv")
                B["ob"].add_write(evn2)
                fB.note_read(evn2)
                B["bank"][bb0].note_read(evn2)
                B["bank"][bb0 + 1].note_read(evn2)
            finq.append((fin_a, fin_b))
        while finq:
            a, b = finq.pop(0)
            a()
            b()

        for grp in range(2):
            fms = list(range(8 * grp, 8 * grp + 8))
            nho = 6 if grp == 0 else HPN
            for ho in range(nho):
                rsl = ho % 2
                rB = B["ring"][rsl]
                wst = A(ring_o + rsl * 2048, 2048)
                evl = dma("sync", "or%d" % rsl, wst, wospill[ho],
                          waits=rB.wr_waits() + B["wospill"].rd_waits())
                rB.note_write(evl)
                evx = None
                for i, fm in enumerate(fms):
                    nho_f = 2 if fm < 2 else (6 if fm < 8 else HPN)
                    if ho >= nho_f:
                        continue
                    w = list(rB.rd_waits())
                    if ho == 0:
                        w += B["ob"].rd_waits() + B["bank"][i].wr_waits()
                    evx = kb.emit(
                        "tensor",
                        lambda e, i=i, fm=fm, ho=ho, wst=wst,
                               nho_f=nho_f: e.matmul(
                            bank(i), wst[:, fm * 128:(fm + 1) * 128],
                            ob[:, ho * 512:(ho + 1) * 512],
                            start=(ho == 0), stop=(ho == nho_f - 1)),
                        waits=w, inc="sp")
                    if ho == nho_f - 1:
                        B["bank"][i].note_write(evx)
                        B["ob"].note_read(evx)
                if evx is not None:
                    rB.note_read(evx)
            for i, fm in enumerate(fms):
                evr = kb.emit(
                    "vector",
                    lambda e, i=i, fm=fm: e.tensor_tensor(
                        xos[:, fm * 512:(fm + 1) * 512], bank(i),
                        xos[:, fm * 512:(fm + 1) * 512], ALU.add),
                    waits=B["bank"][i].rd_waits() + B["xos"].wr_waits(), inc="sv")
                B["xos"].add_write(evr)
                B["bank"][i].note_read(evr)

        _emit_ln(kb, B, bank,
                 lambda fm: xos[:, fm * 512:(fm + 1) * 512], B["xos"],
                 lambda fm, s=s: x1[:, fm * 2048 + s * 512: fm * 2048 + s * 512 + 512],
                 ering, sm32, smf, rowb, onesc, onesr, out_f32=None, ostg=None,
                 dma=dma, out_dma=None,
                 tmpA=lambda i: A(86144 + 1024 + i * 512, 512), tmpB=B["efin"])

    kb.raw_wait("sync", [(n, kb.cnt[n]) for n in ("k0", "k1", "v0", "v1", "mk",
                                                   "ql", "or0", "or1")])
    kb.raw_wait("gpsimd", [(n, kb.cnt[n]) for n in ("xos", "wr0", "wr1", "wr2", "wr3")])
    kb.barrier()

    r2 = A(r2_o, NCK * 1024)
    ln2q = []
    for blk in range(2):
        if blk == 1 and ln2q:
            pass
        for q16 in range(16):
            if blk == 1 and q16 == 4 and ln2q:
                ln2q.pop(0)()
            for hq2 in range(2):
                tiles = (q16 + 32 * hq2, q16 + 16 + 32 * hq2)
                for ti, tile in enumerate(tiles):
                    bb = 4 * ((2 * hq2 + ti) % 2)
                    gB, uB = B["bank"][bb], B["bank"][bb + 2]
                    g2B, u2B = B["bank"][bb + 1], B["bank"][bb + 3]
                    rg, ru = 2 * (ti % 2), 2 * (ti % 2) + 1
                    rgB, ruB = B["ring"][rg], B["ring"][ru]
                    wgs = A(ring2_o + rg * 2048, 2048)
                    wus = A(ring2_o + ru * 2048, 2048)
                    evg = dma("gpsimd", "wr%d" % rg, wgs, wg[tile],
                              waits=rgB.wr_waits())
                    rgB.note_write(evg)
                    evu = dma("gpsimd", "wr%d" % ru, wus, wu[tile],
                              waits=ruB.wr_waits())
                    ruB.note_write(evu)
                    for kc in range(NCK):
                        for th in range(2):
                            s = 2 * blk + th
                            w = list(rgB.rd_waits())
                            if kc == 0 and th == 0:
                                w += (B["x1"].rd_waits() + gB.wr_waits()
                                      + g2B.wr_waits() + uB.wr_waits()
                                      + u2B.wr_waits())
                            kb.emit("tensor",
                                    lambda e, bb=bb, th=th, kc=kc, s=s, wgs=wgs: e.matmul(
                                        bank(bb + th),
                                        wgs[:, kc * 128:(kc + 1) * 128],
                                        x1[:, kc * 2048 + s * 512: kc * 2048 + s * 512 + 512],
                                        start=(kc == 0), stop=(kc == NCK - 1)),
                                    waits=w, inc=("sp" if kc == NCK - 1 else None))
                        for th in range(2):
                            s = 2 * blk + th
                            evm = kb.emit(
                                "tensor",
                                lambda e, bb=bb, th=th, kc=kc, s=s, wus=wus: e.matmul(
                                    bank(bb + 2 + th),
                                    wus[:, kc * 128:(kc + 1) * 128],
                                    x1[:, kc * 2048 + s * 512: kc * 2048 + s * 512 + 512],
                                    start=(kc == 0), stop=(kc == NCK - 1)),
                                waits=list(ruB.rd_waits()),
                                inc=("sp" if (kc == NCK - 1 or th == 1) else None))
                        rgB.note_read(evm)
                        ruB.note_read(evm)
                    gB.note_write(("sp", evm[1] - 2))
                    g2B.note_write(("sp", evm[1] - 2))
                    uB.note_write(("sp", evm[1] - 1))
                    u2B.note_write(evm)
                    B["x1"].note_read(evm)
                    tb = A(t2_o, 1024)
                    t2 = A(t2_o + 1024, 1024)
                    at = A(t2_o + 2048, 1024)
                    eB = B["e"][0]
                    gw = [("sp", evm[1] - 2)]
                    evt = kb.emit("scalar",
                                  lambda e, bb=bb, tb=tb: e.activation(
                                      tb, ps[:, bb * 512:(bb + 2) * 512],
                                      AF.Tanh, scale=0.5),
                                  waits=gw + eB.wr_waits(), inc="sa")
                    evt2 = kb.emit("vector",
                                   lambda e, tb=tb, t2=t2: e.tensor_scalar(
                                       t2, tb, 0.5, 0.5, ALU.mult, ALU.add),
                                   waits=[("sa", evt[1])], inc="sv")
                    eva = kb.emit("vector",
                                  lambda e, bb=bb, t2=t2, at=at: e.tensor_tensor(
                                      at, ps[:, bb * 512:(bb + 2) * 512], t2, ALU.mult),
                                  waits=[("sv", evt2[1])] + gw, inc="sv")
                    gB.note_read(eva)
                    g2B.note_read(eva)
                    hqi = ti + 2 * hq2
                    evh = kb.emit("vector",
                                  lambda e, bb=bb, at=at, hqi=hqi: e.tensor_tensor(
                                      A(hq_o + hqi * 1024, 1024), at,
                                      ps[:, (bb + 2) * 512:(bb + 4) * 512], ALU.mult),
                                  waits=[("sv", eva[1]), ("sp", evm[1])]
                                        + B["hq"].wr_waits(), inc="sv")
                    uB.note_read(evh)
                    u2B.note_read(evh)
                    B["hq"].add_write(evh)
                    eB.note_write(evh)
            aoff = [t2_o + 4096 + i * 1024 for i in range(4)]
            n2off = t2_o + 3072
            hql = lambda i: A(hq_o + i * 1024, 1024)
            absl = lambda i: A(aoff[i], 1024)
            wsp = B["hq"].rd_waits() + B["e"][0].wr_waits() + B["e"][1].wr_waits()
            last = None
            for i in range(4):
                last = kb.emit("vector",
                               lambda e, i=i, absl=absl, hql=hql: e.scalar_tensor_tensor(
                                   absl(i), hql(i), -1.0, hql(i), ALU.mult, ALU.max),
                               waits=wsp if i == 0 else [("sv", last[1])], inc="sv")
            m1, m2, n1, n2 = (A(t2_o, 1024), A(t2_o + 1024, 1024),
                              A(t2_o + 2048, 1024), A(n2off, 1024))
            seq = [
                lambda e: e.tensor_tensor(m1, absl(0), absl(1), ALU.max),
                lambda e: e.tensor_tensor(m2, absl(2), absl(3), ALU.max),
                lambda e: e.tensor_tensor(n1, absl(0), absl(1), ALU.min),
                lambda e: e.tensor_tensor(n2, absl(2), absl(3), ALU.min),
                lambda e: e.tensor_tensor(m1, m1, m2, ALU.min),
                lambda e: e.tensor_tensor(n1, n1, n2, ALU.max),
                lambda e: e.tensor_tensor(m1, m1, n1, ALU.max),
            ]
            for f in seq:
                last = kb.emit("vector", f, waits=[("sv", last[1])], inc="sv")
            for i in range(4):
                gekeep = kb.emit("vector",
                                 lambda e, i=i, absl=absl: e.tensor_tensor(
                                     absl(i), absl(i), m1, ALU.is_ge),
                                 waits=[("sv", last[1])], inc="sv")
                tile = q16 + 16 * i
                hwB = B["hsw"][i]
                hsv = kb.emit("vector",
                              lambda e, i=i, absl=absl, hql=hql: e.tensor_tensor(
                                  A(hsw_o + i * 1024, 1024), absl(i), hql(i),
                                  ALU.mult),
                              waits=[("sv", gekeep[1])] + hwB.wr_waits(), inc="sv")
                hwB.note_write(hsv)
                last = hsv
                evd = dma("sync", "hs%d" % i,
                          hspill[tile, :, blk * 1024:(blk + 1) * 1024],
                          A(hsw_o + i * 1024, 1024), waits=hwB.rd_waits())
                hwB.note_read(evd)
                B["hspill"].add_write(evd)
            B["hq"].note_read(last)
            B["e"][0].note_write(last)
            B["e"][1].note_write(last)

        for fq in range(4):
            evx = None
            for ic in range(64):
                rsl = ic % 6
                rB = B["wdr"][rsl]
                wst = A(78848 + rsl * 512, 512)
                evl = dma("gpsimd", "wd%d" % rsl,
                          wst, wd[ic * 128:(ic + 1) * 128, fq * 512:(fq + 1) * 512],
                          waits=rB.wr_waits())
                rB.note_write(evl)
                hsl = ic % 6
                hB = B["hst"][hsl]
                hst = A((hst_o + hsl * 1024) if hsl < 4 else (81920 + (hsl - 4) * 1024), 1024)
                evhl = dma("sync", "hr%d" % hsl, hst,
                           hspill[ic, :, blk * 1024:(blk + 1) * 1024],
                           waits=hB.wr_waits() + B["hspill"].rd_waits())
                hB.note_write(evhl)
                for ft in range(4):
                    for th in range(2):
                        bi = 2 * ft + th
                        w = list(rB.rd_waits()) + hB.rd_waits()
                        if ic == 0:
                            w += B["bank"][bi].wr_waits()
                        evx = kb.emit(
                            "tensor",
                            lambda e, bi=bi, ft=ft, th=th, ic=ic, wst=wst, hst=hst: e.matmul(
                                bank(bi), wst[:, ft * 128:(ft + 1) * 128],
                                hst[:, th * 512:(th + 1) * 512],
                                start=(ic == 0), stop=(ic == 63)),
                            waits=w,
                            inc=("sp" if (ic == 63 or (ft == 3 and th == 1)) else None))
                        if ic == 63:
                            B["bank"][bi].note_write(evx)
                rB.note_read(evx)
                hB.note_read(evx)
                if ic == 63:
                    B["hspill"].note_read(evx)
            for ft in range(4):
                for th in range(2):
                    bi = 2 * ft + th
                    fm = 4 * fq + ft
                    evr = kb.emit(
                        "vector",
                        lambda e, bi=bi, fm=fm, th=th, blk=blk: e.tensor_tensor(
                            r2[:, fm * 1024 + th * 512: fm * 1024 + th * 512 + 512],
                            bank(bi),
                            x1[:, fm * 2048 + (2 * blk + th) * 512:
                               fm * 2048 + (2 * blk + th) * 512 + 512],
                            ALU.add),
                        waits=B["bank"][bi].rd_waits() + B["x1"].rd_waits()
                              + B["r2"].wr_waits(), inc="sv")
                    B["r2"].add_write(evr)
                    B["bank"][bi].note_read(evr)
                    B["x1"].note_read(evr)

        def _ln2(blk=blk):
            for th in range(2):
                s = 2 * blk + th
                _emit_ln(kb, B, bank,
                         lambda fm, th=th: r2[:, fm * 1024 + th * 512:
                                              fm * 1024 + th * 512 + 512],
                         B["r2"], None, ering, sm32, smf, rowb, onesc, onesr,
                         out_f32=True, ostg=ostg, dma=dma,
                         out_dma=lambda fm, s=s: outT[fm * 128:(fm + 1) * 128,
                                                      s * 512:(s + 1) * 512],
                         tmpA=lambda i: A(77824 + i * 512, 512),
                         tmpB=B["e"][1])
        if blk == 1:
            ln2q.append(_ln2)
            for f in ln2q:
                f()
        else:
            ln2q.append(_ln2)

    kb.raw_wait("sync", [("so0", kb.cnt["so0"]), ("so1", kb.cnt["so1"]),
                         ("hs0", kb.cnt["hs0"]), ("hs1", kb.cnt["hs1"]),
                         ("hs2", kb.cnt["hs2"]), ("hs3", kb.cnt["hs3"]),
                         ("hr0", kb.cnt["hr0"]), ("hr1", kb.cnt["hr1"]),
                         ("hr2", kb.cnt["hr2"]), ("hr3", kb.cnt["hr3"])])

    with nc.Block() as block:
        @block.tensor
        def _(t):
            for fn in kb.ops["tensor"]:
                fn(t)

        @block.vector
        def _(v):
            for fn in kb.ops["vector"]:
                fn(v)

        @block.scalar
        def _(sc):
            for fn in kb.ops["scalar"]:
                fn(sc)

        @block.sync
        def _(sy):
            for fn in kb.ops["sync"]:
                fn(sy)

        @block.gpsimd
        def _(g):
            for fn in kb.ops["gpsimd"]:
                fn(g)

    return nc, st


def _emit_ln(kb, B, bank, src_slice, srcB, x1_slice, ering, sm32, smf, rowb,
             onesc, onesr, out_f32, ostg, dma, out_dma, tmpA=None, tmpB=None):
    b0B, b1B = B["bank"][0], B["bank"][1]
    sq = tmpA(0)
    eB = tmpB
    evs2 = None
    for fm in range(16):
        w = list(srcB.rd_waits())
        if fm == 0:
            w += b0B.wr_waits()
        ev1 = kb.emit("tensor",
                      lambda e, fm=fm: e.matmul(
                          bank(0, rows=1), onesc[:, 0:1],
                          src_slice(fm),
                          start=(fm == 0), stop=(fm == 15)),
                      waits=w + B["ones"].rd_waits(),
                      inc=("sp" if fm == 15 else None))
        evq = kb.emit("vector",
                      lambda e, fm=fm, sq=sq: e.tensor_tensor(
                          sq, src_slice(fm),
                          src_slice(fm), ALU.mult),
                      waits=list(srcB.rd_waits()) + eB.wr_waits(), inc="sv")
        eB.note_write(evq)
        w = [("sv", evq[1])]
        if fm == 0:
            w += b1B.wr_waits()
        evs2 = kb.emit("tensor",
                       lambda e, fm=fm, sq=sq: e.matmul(
                           bank(1, rows=1), onesc[:, 0:1], sq,
                           start=(fm == 0), stop=(fm == 15)),
                       waits=w + B["ones"].rd_waits(), inc="sp")
        eB.note_read(evs2)
        if fm == 15:
            b0B.note_write(evs2)
            b1B.note_write(evs2)
            srcB.note_read(evs2)
            B["ones"].note_read(evs2)
    smB = B["sm"]
    ev = kb.emit("vector",
                 lambda e: e.tensor_scalar(sm32[0:1, 0:512], bank(0, rows=1),
                                           1.0 / 2048, None, ALU.mult),
                 waits=b0B.rd_waits() + smB.wr_waits(), inc="sv")
    b0B.note_read(ev)
    ev = kb.emit("vector",
                 lambda e: e.tensor_scalar(sm32[0:1, 512:1024], bank(1, rows=1),
                                           1.0 / 2048, None, ALU.mult),
                 waits=b1B.rd_waits(), inc="sv")
    b1B.note_read(ev)
    ev = kb.emit("vector",
                 lambda e: e.tensor_tensor(sm32[0:1, 1024:1536], sm32[0:1, 0:512],
                                           sm32[0:1, 0:512], ALU.mult),
                 waits=[("sv", ev[1])], inc="sv")
    ev = kb.emit("vector",
                 lambda e: e.tensor_tensor(sm32[0:1, 512:1024], sm32[0:1, 512:1024],
                                           sm32[0:1, 1024:1536], ALU.subtract),
                 waits=[("sv", ev[1])], inc="sv")
    ev = kb.emit("vector",
                 lambda e: e.tensor_scalar(sm32[0:1, 512:1024], sm32[0:1, 512:1024],
                                           EPS, None, ALU.add),
                 waits=[("sv", ev[1])], inc="sv")
    ev = kb.emit("scalar",
                 lambda e: e.activation(sm32[0:1, 1024:1536], sm32[0:1, 512:1024],
                                        AF.Ln),
                 waits=[("sv", ev[1])], inc="sa")
    ev = kb.emit("scalar",
                 lambda e: e.activation(sm32[0:1, 1536:2048], sm32[0:1, 1024:1536],
                                        AF.Exp, scale=-0.5),
                 waits=[("sa", ev[1])], inc="sa")
    ev = kb.emit("vector",
                 lambda e: e.tensor_tensor(sm32[0:1, 1024:1536], sm32[0:1, 0:512],
                                           sm32[0:1, 1536:2048], ALU.mult),
                 waits=[("sa", ev[1])], inc="sv")
    rB = B["row"]
    ev = kb.emit("vector",
                 lambda e: e.tensor_copy(rowb[0:1, 0:512], sm32[0:1, 1536:2048]),
                 waits=[("sv", ev[1])] + rB.wr_waits(), inc="sv")
    ev = kb.emit("vector",
                 lambda e: e.tensor_copy(rowb[0:1, 512:1024], sm32[0:1, 1024:1536]),
                 waits=[("sv", ev[1])], inc="sv")
    rB.note_write(ev)
    smB.note_write(ev)
    b2B, b3B = B["bank"][2], B["bank"][3]
    evb = kb.emit("tensor",
                  lambda e: e.matmul(bank(2), onesr[0:1, :], rowb[0:1, 0:512],
                                     start=True, stop=True),
                  waits=rB.rd_waits() + B["onesr"].rd_waits()
                        + b2B.wr_waits() + b3B.wr_waits(), inc="sp")
    evb2 = kb.emit("tensor",
                   lambda e: e.matmul(bank(3), onesr[0:1, :], rowb[0:1, 512:1024],
                                      start=True, stop=True),
                   waits=(), inc="sp")
    b2B.note_write(evb)
    b3B.note_write(evb2)
    rB.note_read(evb2)
    B["onesr"].note_read(evb2)
    smfB = B["smf"]
    for fm in range(16):
        if out_f32:
            ev1 = kb.emit("vector",
                          lambda e, fm=fm: e.tensor_tensor(
                              smf[:, :], src_slice(fm),
                              bank(2), ALU.mult),
                          waits=list(srcB.rd_waits()) + [("sp", evb2[1])]
                                + smfB.wr_waits(), inc="sv")
            smfB.note_write(ev1)
            so = fm % 2
            oB = B["ostg"][so]
            ev2 = kb.emit("vector",
                          lambda e, fm=fm, so=so: e.tensor_tensor(
                              ostg[:, so * 512:(so + 1) * 512], smf[:, :],
                              bank(3), ALU.subtract),
                          waits=[("sv", ev1[1])] + oB.wr_waits(), inc="sv")
            smfB.note_read(ev2)
            oB.note_write(ev2)
            evd = dma("sync", "so%d" % so, out_dma(fm),
                      ostg[:, so * 512:(so + 1) * 512], waits=oB.rd_waits())
            oB.note_read(evd)
            b2B.note_read(ev2)
            b3B.note_read(ev2)
            srcB.note_read(ev2)
        else:
            tmp = tmpA(1)
            ev1 = kb.emit("vector",
                          lambda e, fm=fm, tmp=tmp: e.tensor_tensor(
                              tmp, src_slice(fm),
                              bank(2), ALU.mult),
                          waits=list(srcB.rd_waits()) + [("sp", evb2[1])]
                                + tmpB.wr_waits(), inc="sv")
            tmpB.note_write(ev1)
            dst = x1_slice(fm)
            ev2 = kb.emit("vector",
                          lambda e, dst=dst, tmp=tmp: e.tensor_tensor(
                              dst, tmp, bank(3), ALU.subtract),
                          waits=[("sv", ev1[1])] + B["x1"].wr_waits(), inc="sv")
            B["x1"].add_write(ev2)
            tmpB.note_read(ev2)
            b2B.note_read(ev2)
            b3B.note_read(ev2)
            srcB.note_read(ev2)



_PERM = None


def _perm():
    global _PERM
    if _PERM is None:
        p = np.arange(FF)
        _PERM = 4 * (p % 2048) + (p // 2048)
    return _PERM


def _wtiles(w):
    r = w.reshape(NCK, 128, 64, 128).transpose(2, 1, 0, 3).reshape(64, 128, NCK * 128)
    return np.ascontiguousarray(r)


def _blockdiag(P):
    nh, r, dk = P.shape
    out = np.zeros((nh * r, nh * dk), np.float32)
    for h in range(nh):
        out[h * r:(h + 1) * r, h * dk:(h + 1) * dk] = P[h]
    return out


def _masks(parity):
    out = np.zeros((8, 128, 2048), np.float32)
    for s in range(4):
        g = TILES[parity][s]
        ext = EXT[s]
        for idx in range(2):
            j = ext - 2 + idx
            for sub in range(4):
                k = 512 * j + 128 * sub + np.arange(128)[:, None]
                q = 512 * g + np.arange(512)[None, :]
                out[2 * s + idx, :, 512 * sub:512 * (sub + 1)] = (k <= q)
    return out.astype(ml_dtypes.bfloat16)


_CACHE = {}


def prep_inputs(L0_attn_x, L0_attn_W_Q, L0_attn_W_K, L0_attn_W_V, L0_attn_W_O,
                L0_attn_FK0, L0_attn_PK0, L0_attn_FV0, L0_attn_PV0,
                L0_attn_FK1, L0_attn_PK1, L0_attn_FV1, L0_attn_PV1,
                L0_ffn_w_gate, L0_ffn_w_up, L0_ffn_w_down):
    x = np.ascontiguousarray(np.asarray(L0_attn_x, np.float32))
    perm = _perm()
    shared = dict(
        wq=np.ascontiguousarray(np.asarray(L0_attn_W_Q, np.float32)),
        wk=np.ascontiguousarray(np.asarray(L0_attn_W_K, np.float32)),
        wv=np.ascontiguousarray(np.asarray(L0_attn_W_V, np.float32)),
        wo=np.ascontiguousarray(np.asarray(L0_attn_W_O, np.float32)),
        fk0=np.ascontiguousarray(np.asarray(L0_attn_FK0, np.float32)),
        fv0=np.ascontiguousarray(np.asarray(L0_attn_FV0, np.float32)),
        fk1=np.ascontiguousarray(np.asarray(L0_attn_FK1, np.float32)),
        fv1=np.ascontiguousarray(np.asarray(L0_attn_FV1, np.float32)),
        bk0=_blockdiag(np.asarray(L0_attn_PK0, np.float32)),
        bv0=_blockdiag(np.asarray(L0_attn_PV0, np.float32)),
        bk1=_blockdiag(np.asarray(L0_attn_PK1, np.float32)),
        bv1=_blockdiag(np.asarray(L0_attn_PV1, np.float32)),
        wg=_wtiles(np.asarray(L0_ffn_w_gate, np.float32)[:, perm]),
        wu=_wtiles(np.asarray(L0_ffn_w_up, np.float32)[:, perm]),
        wd=np.ascontiguousarray(np.asarray(L0_ffn_w_down, np.float32)[perm, :]),
    )
    mks = [_masks(0), _masks(1)]
    in_maps = []
    for c in range(8):
        b, par = c // 2, c % 2
        xb = x[b]
        xTf = np.ascontiguousarray(xb.T)
        own = np.concatenate([xb[512 * g:512 * (g + 1)] for g in TILES[par]], 0)
        in_maps.append(dict(shared, xT=xTf,
                            xTo=np.ascontiguousarray(own.T), msk=mks[par]))
    return in_maps


def kernel(**inputs):
    in_maps = prep_inputs(**inputs)
    if "nc" not in _CACHE:
        nc, st = build_program()
        _CACHE["nc"] = nc
        _CACHE["st"] = st
    res = run_bass_kernel_spmd(_CACHE["nc"], in_maps, core_ids=list(range(8)))
    return assemble(res.results)


def assemble(results):
    out = np.empty((4, T, D), np.float32)
    for c in range(8):
        b, par = c // 2, c % 2
        oT = np.asarray(results[c]["outT"])
        o = np.ascontiguousarray(oT.T)
        for s, g in enumerate(TILES[par]):
            out[b, 512 * g:512 * (g + 1)] = o[512 * s:512 * (s + 1)]
    return out
